# version 13
# baseline (speedup 1.0000x reference)
import numpy as np
import concourse.bass as bass
import concourse.mybir as mybir
from concourse.bass_utils import run_bass_kernel_spmd

F32 = mybir.dt.float32
BF16 = mybir.dt.bfloat16
AF = mybir.ActivationFunctionType
ALU = mybir.AluOpType
AX = mybir.AxisListType

B, N, D = 4, 4096, 1024
DFF, NEXP = 4096, 8
TOK = 2048
KC = 8
PAIRS = 8
KT = 32
QCN, QW = 4, 512
NT = 16
FQN = 8
TCN = 4


class Sem:
    def __init__(self, nc, name):
        self.h = nc.alloc_semaphore(name)
        self.n = 0

    def bump(self, k=1):
        self.n += k
        return self.n


class Plan:
    def __init__(self):
        self.q = {k: [] for k in ("t", "v", "s", "g", "y")}
        self.guards = {}
        self.last_wait = {k: {} for k in ("t", "v", "s", "g", "y")}

    def emit(self, eng, build, sem=None, k=1):
        if sem is None:
            self.q[eng].append(build)
            return None
        v = sem.bump(k)
        h = sem.h
        self.q[eng].append(lambda e, b=build, h=h, k=k: b(e).then_inc(h, k))
        return v

    def wait(self, eng, sem, val):
        if val and val > 0:
            lw = self.last_wait[eng]
            if lw.get(id(sem), -1) >= val:
                return
            lw[id(sem)] = val
            self.q[eng].append(lambda e, s=sem.h, v=val: e.wait_ge(s, v))

    def use(self, eng, res):
        g = self.guards.get(res)
        if g is not None:
            self.wait(eng, g[0], g[1])

    def guard(self, res, sem, val=None):
        self.guards[res] = (sem, sem.n if val is None else val)


def build_nc(stop=9):
    nc = bass.Bass("TRN2", target_bir_lowering=False)

    xkv = nc.dram_tensor("xkv", [N, D], F32, kind="ExternalInput").ap()
    wqkv_t = nc.dram_tensor("wqkv_t", [128, PAIRS, 3, KC, 128], F32, kind="ExternalInput").ap()
    wo_t = nc.dram_tensor("wo_t", [128, KC * 8 * 128], F32, kind="ExternalInput").ap()
    bq_c = nc.dram_tensor("bq_c", [128, 8], F32, kind="ExternalInput").ap()
    bk_c = nc.dram_tensor("bk_c", [128, 8], F32, kind="ExternalInput").ap()
    bo_c = nc.dram_tensor("bo_c", [128, 8], F32, kind="ExternalInput").ap()
    bv_row = nc.dram_tensor("bv_row", [1, D], F32, kind="ExternalInput").ap()
    ident32 = nc.dram_tensor("ident32", [128, 128], F32, kind="ExternalInput").ap()
    gt = nc.dram_tensor("gt", [NEXP, 128, KC, DFF], F32, kind="ExternalInput").ap()
    ut = nc.dram_tensor("ut", [NEXP, 128, KC, DFF], F32, kind="ExternalInput").ap()
    dtw = nc.dram_tensor("dtw", [NEXP, 128, 32, D], F32, kind="ExternalInput").ap()
    rw_t = nc.dram_tensor("rw_t", [128, KC, 7], F32, kind="ExternalInput").ap()
    out = nc.dram_tensor("out", [TOK, D], F32, kind="ExternalOutput").ap()
    rscr = nc.dram_tensor("rscr", [8, QW], F32, kind="Internal").ap()
    comb_d = nc.dram_tensor("comb_d", [7, TOK], BF16, kind="Internal").ap()

    al = nc.alloc_sbuf_tensor
    id32 = al("id32", [128, 128], F32)
    idbf = al("idbf", [128, 128], BF16)
    bq_s = al("bq_s", [128, 8], F32)
    bk_s = al("bk_s", [128, 8], F32)
    bo_s = al("bo_s", [128, 8], F32)
    bvr_s = al("bvr_s", [1, D], BF16)
    ones_c = al("ones_c", [128, 1], BF16)
    ones_r = al("ones_r", [1, 128], BF16)
    ones32r = al("ones32r", [1, 128], F32)
    rw_s = al("rw_s", [128, KC, 8], BF16)
    lgs = al("lgs", [7, QW], F32)
    lgt = al("lgt", [128, NT, 8], F32)
    prb = al("prb", [128, NT, 8], F32)
    msk = al("msk", [128, NT, 8], F32)
    scs = al("scs", [128, NT, 8], F32)
    comb = al("comb", [128, NT, 8], F32)
    combT = al("combT", [7, NT * 128], BF16)
    rowA = al("rowA", [1, QW], F32)
    rowB = al("rowB", [1, QW], F32)
    rowT = al("rowT", [1, QW], F32)
    nm_s = al("nm_s", [128, 1], F32)
    nm_r = al("nm_r", [1, 1], F32)
    nm_rb = al("nm_rb", [1, 1], BF16)
    mh_s = al("mh_s", [1, 16], F32)
    eps_s = al("eps_s", [128, 1], F32)
    padv = al("padv", [128, 8], F32)
    pads = al("pads", [128, 8], F32)
    abc_s = al("abc_s", [128, 2, QW], F32)
    stg = al("stg", [128, QW], BF16)

    a0 = nc.sbuf_base
    attnT = al("attnT", [128, PAIRS, TOK], BF16)
    xT = al("xT", [128, 2, KC, TOK], BF16)
    p0_end = nc.sbuf_base
    assert p0_end - a0 == 98304, (a0, p0_end)
    nc.sbuf_base = a0
    acc = al("acc", [128, KC, TOK], F32)
    nc.sbuf_base = a0 + 65536
    x1T = al("x1T", [128, KC, TOK], BF16)
    nc.sbuf_base = p0_end

    b0 = nc.sbuf_base
    K_p = al("K_p", [128, N], BF16)
    V_p = al("V_p", [128, KT, 130], BF16)
    q_p = al("q_p", [128, TOK], BF16)
    wqkv_s = al("wqkv_s", [128, 2, 3, KC, 128], BF16)
    es = al("es", [128, 4, 2 * QW], BF16)
    yres_off = nc.sbuf_base
    yres = al("yres", [128, 2, 8, QW], BF16)
    nc.sbuf_base = yres_off
    sq2 = al("sq2", [128, TOK], BF16)
    nc.sbuf_base = yres_off + 16384
    y2 = al("y2", [128, QW], BF16)
    den_s = al("den_s", [128, QW], F32)
    rcpb = al("rcpb", [128, 2, QW], F32)
    xload = al("xload", [128, 2, D], BF16)
    b_end = nc.sbuf_base
    nc.sbuf_base = b0
    y2f = al("y2f", [128, 8, 512], BF16)

    nc.sbuf_base = b0
    wg = al("wg", [128, 2, KC, 512], BF16)
    wu = al("wu", [128, 2, KC, 512], BF16)
    wd = al("wd", [128, 2, 4, D], BF16)
    cb_all = al("cb_all", [128, 2, TCN, 512], BF16)
    hbuf = al("hbuf", [128, 2, 4, 512], BF16)
    s1b = al("s1b", [128, 4, 512], BF16)
    t1b = al("t1b", [128, 4, 512], BF16)
    c_end = nc.sbuf_base
    nc.sbuf_base = b0
    zbuf = al("zbuf", [128, 2, 8, QW], BF16)
    z2f = al("z2f", [128, 8, QW], BF16)
    x2qc = al("x2qc", [128, 8, QW], F32)
    outb = al("outb", [128, 2, D], F32)
    d_end = nc.sbuf_base
    nc.sbuf_base = max(b_end, c_end, d_end)
    wo_s = al("wo_s", [128, KC, 8, 128], BF16)
    assert nc.sbuf_base <= nc.sbuf_top, (nc.sbuf_base, nc.sbuf_top)

    pz = nc.psum_base
    pb = [nc.alloc_psum_tensor(f"pb{i}", [128, 512], F32) for i in range(8)]
    nc.psum_base = pz
    pTb = nc.alloc_psum_tensor("pTb", [128, 4096], BF16)
    nc.psum_base = pz
    pT32 = nc.alloc_psum_tensor("pT32", [128, 1024], F32)
    nc.psum_base = pz
    pS2 = [nc.alloc_psum_tensor(f"pS2_{i}", [128, 1024], F32) for i in range(2)]
    nc.psum_base = pz + 8

    dX = [Sem(nc, "dX0"), Sem(nc, "dX1")]
    dWm = Sem(nc, "dWm"); dWc = Sem(nc, "dWc")
    dWq = [Sem(nc, "dWq0"), Sem(nc, "dWq1")]
    dWg = [Sem(nc, "dWg0"), Sem(nc, "dWg1")]
    dWu = [Sem(nc, "dWu0"), Sem(nc, "dWu1")]
    dWd = [Sem(nc, "dWd0"), Sem(nc, "dWd1")]
    dR = Sem(nc, "dR"); dO = [Sem(nc, "dO0"), Sem(nc, "dO1")]
    tA = Sem(nc, "tA"); eA = Sem(nc, "eA")
    bK = Sem(nc, "bK"); fK = Sem(nc, "fK")
    bS = Sem(nc, "bS"); fS = Sem(nc, "fS")
    avp = Sem(nc, "avp"); fAV = Sem(nc, "fAV")
    bY = Sem(nc, "bY"); fY = Sem(nc, "fY")
    bST = Sem(nc, "bST"); vM = Sem(nc, "vM"); sM = Sem(nc, "sM")
    fX1 = Sem(nc, "fX1")
    bG = Sem(nc, "bG"); bU = Sem(nc, "bU"); sS1 = Sem(nc, "sS1"); fH = Sem(nc, "fH")
    bO = Sem(nc, "bO"); fO = Sem(nc, "fO")
    vX2 = Sem(nc, "vX2"); eT = Sem(nc, "eT")

    P = Plan()

    def vfence():
        P.wait("v", vM, vM.n)

    def sfence():
        P.wait("s", sM, sM.n)

    def g_dma(dst, src, sem):
        return P.emit("g", lambda g, d=dst, s=src: g.dma_start(out=d, in_=s), sem, 16)

    P.emit("y", lambda y: y.dma_start(out=id32[:], in_=ident32), dR, 16)
    P.emit("y", lambda y: y.dma_start(out=bq_s[:], in_=bq_c), dR, 16)
    P.emit("y", lambda y: y.dma_start(out=bk_s[:], in_=bk_c), dR, 16)
    P.emit("y", lambda y: y.dma_start(out=bo_s[:], in_=bo_c), dR, 16)
    misc_dR = dR.n
    g_dma(idbf[:], ident32, dWm)
    g_dma(bvr_s[:], bv_row, dWm)
    g_dma(rw_s[:, :, 0:7], rw_t, dWm)
    g_dma(bass.AP(wo_s, 0, [[8192, 128], [1, 8192]]), wo_t, dWm)
    misc_dW = dWm.n
    P.emit("v", lambda v: v.memset(eps_s[:], 1e-5))
    P.emit("v", lambda v: v.memset(ones_c[:], 1.0))
    P.emit("v", lambda v: v.memset(ones_r[:], 1.0))
    P.emit("v", lambda v: v.memset(ones32r[:], 1.0))
    P.emit("v", lambda v: v.memset(V_p[:, :, 64:65], 1.0))
    P.emit("v", lambda v: v.memset(V_p[:, :, 129:130], 1.0), vM)
    misc_vM = vM.n

    if stop >= 1:
        phase_a_gate = True
    ntr = 0
    for tt in range(32 if stop >= 1 else 0):
        if tt >= 2:
            P.wait("g", eA, 8 * (tt - 1))
        P.emit("g", lambda g, tt=tt: g.dma_start(
            out=xload[:, tt % 2, :], in_=xkv[tt * 128:(tt + 1) * 128, :]), dX[tt % 2], 16)
        for kc in range(KC):
            n = ntr
            P.wait("t", dX[tt % 2], 16 * (tt // 2 + 1))
            if n == 0:
                P.wait("t", dR, misc_dR)
                P.wait("t", dWm, misc_dW)
            if n >= 4:
                P.wait("t", eA, n - 3)
            po = (n % 4) * 1024
            P.emit("t", lambda t, po=po, tt=tt, kc=kc: t.transpose(
                pTb[:, po:po + 128], xload[:, tt % 2, kc * 128:(kc + 1) * 128],
                idbf[:]), tA)
            P.wait("s", tA, n + 1)
            P.emit("s", lambda s, po=po, tt=tt, kc=kc: s.copy(
                out=xT[:, tt // 16, kc, (tt % 16) * 128:(tt % 16 + 1) * 128],
                in_=pTb[:, po:po + 128]), eA)
            ntr += 1
    P.guard("bank0", eA)
    P.guard("bank1", eA)
    P.guard("bank2", eA)
    P.guard("bank3", eA)

    def roundtrip(slot, src_key, src_row, dst_tile, width, src_sem, src_val, dst_key):
        P.wait("y", src_sem, src_val)
        P.wait("y", dR, dR.n)
        P.use("y", ("rslot", slot))
        P.emit("y", lambda y, sl=slot, sr=src_row, w=width: y.dma_start(
            out=bass.AP(rscr.tensor, sl * QW, [[QW, 1], [1, w]]), in_=sr), dR, 16)
        P.guards[src_key] = (dR, dR.n)
        P.wait("y", dR, dR.n)
        if dst_key is not None:
            P.use("y", dst_key)
        P.emit("y", lambda y, sl=slot, dd=dst_tile, w=width: y.dma_start(
            out=dd, in_=bass.AP(rscr.tensor, sl * QW, [[0, 128], [1, w]])), dR, 16)
        P.guard(("rslot", slot), dR, dR.n)
        P.wait("y", dR, dR.n)
        return dR.n

    for hp in range(PAIRS if stop >= 2 else 0):
        wpar = hp % 2
        P.use("g", ("wqkv_s", wpar))
        g_dma(wqkv_s[:, wpar], wqkv_t[:, hp, :, :, :], dWq[wpar])
        wq_ld = dWq[wpar].n
        if hp == 0:
            P.wait("t", dR, misc_dR)
            P.wait("t", dWm, misc_dW)
            P.wait("t", vM, misc_vM)
            P.wait("t", eA, 256)
        P.wait("t", dWq[wpar], wq_ld)

        for ncn in range(8):
            bank = ncn % 2
            P.use("t", f"bank{bank}")
            for kc in range(KC):
                P.emit("t", lambda t, bank=bank, ncn=ncn, kc=kc, wpar=wpar: t.matmul(
                    pb[bank][:], wqkv_s[:, wpar, 1, kc, :],
                    xT[:, ncn // 4, kc, (ncn % 4) * 512:(ncn % 4 + 1) * 512],
                    start=(kc == 0), stop=(kc == KC - 1)),
                    bK if kc == KC - 1 else None)
            P.wait("s", bK, bK.n)
            P.emit("s", lambda s, bank=bank, ncn=ncn, hp=hp: s.activation(
                out=K_p[:, ncn * 512:(ncn + 1) * 512], in_=pb[bank][:],
                func=AF.Identity, bias=bk_s[:, hp:hp + 1]), fK)
            P.guard(f"bank{bank}", fK)

        for kt in range(KT):
            bank = 2 + kt % 2
            P.use("t", f"bank{bank}")
            for kc in range(KC):
                P.emit("t", lambda t, bank=bank, kt=kt, kc=kc, wpar=wpar: t.matmul(
                    pb[bank][:, 0:128],
                    xT[:, kt // 16, kc, (kt % 16) * 128:(kt % 16 + 1) * 128],
                    wqkv_s[:, wpar, 2, kc, :], start=(kc == 0), stop=False))
            P.emit("t", lambda t, bank=bank, hp=hp: t.matmul(
                pb[bank][:, 0:128], ones_r[:], bvr_s[0:1, hp * 128:(hp + 1) * 128],
                start=False, stop=True), bK)
            P.wait("s", bK, bK.n)
            P.emit("s", lambda s, bank=bank, kt=kt: s.copy(
                out=bass.AP(V_p, kt * 130, [[KT * 130, 128], [65, 2], [1, 64]]),
                in_=pb[bank][:, 0:128]), fK)
            P.guard(f"bank{bank}", fK)

        for qc in range(QCN):
            bank = 4 + qc % 2
            P.use("t", f"bank{bank}")
            for kc in range(KC):
                P.emit("t", lambda t, bank=bank, qc=qc, kc=kc, wpar=wpar: t.matmul(
                    pb[bank][:], wqkv_s[:, wpar, 0, kc, :],
                    xT[:, 0, kc, qc * 512:(qc + 1) * 512],
                    start=(kc == 0), stop=(kc == KC - 1)),
                    bK if kc == KC - 1 else None)
            P.wait("s", bK, bK.n)
            P.emit("s", lambda s, bank=bank, qc=qc, hp=hp: s.activation(
                out=q_p[:, qc * 512:(qc + 1) * 512], in_=pb[bank][:],
                func=AF.Identity, bias=bq_s[:, hp:hp + 1]), fK)
            P.guard(f"bank{bank}", fK)
        P.guard(("wqkv_s", wpar), bK)

        P.wait("v", vM, vM.n)
        P.wait("s", fK, fK.n)
        P.use("s", "sq2")
        P.emit("s", lambda s: s.activation(out=sq2[:], in_=q_p[:], func=AF.Square), sM)
        P.wait("t", sM, sM.n)
        for ch in range(4):
            mb = 6 + ch % 2
            P.use("t", f"bank{mb}")
            P.emit("t", lambda t, ch=ch, mb=mb: t.matmul(
                pb[mb][0:1, :], ones_c[:], sq2[:, ch * 512:(ch + 1) * 512],
                start=True, stop=True), bK)
            P.wait("v", bK, bK.n)
            P.emit("v", lambda v, ch=ch, mb=mb: v.reduce_max(
                out=mh_s[0:1, ch:ch + 1], in_=pb[mb][0:1, :], axis=AX.X), vM)
            P.guard(f"bank{mb}", vM)
        for half in range(2):
            P.wait("s", bK, bK.n)
            P.emit("s", lambda s, half=half: s.activation(
                out=sq2[:], in_=K_p[:, half * 2048:(half + 1) * 2048],
                func=AF.Square), sM)
            P.wait("t", sM, sM.n)
            for ch in range(4):
                mb = 6 + ch % 2
                P.use("t", f"bank{mb}")
                P.emit("t", lambda t, ch=ch, mb=mb: t.matmul(
                    pb[mb][0:1, :], ones_c[:], sq2[:, ch * 512:(ch + 1) * 512],
                    start=True, stop=True), bK)
                P.wait("v", bK, bK.n)
                P.emit("v", lambda v, half=half, ch=ch, mb=mb: v.reduce_max(
                    out=mh_s[0:1, 4 + half * 4 + ch:5 + half * 4 + ch],
                    in_=pb[mb][0:1, :], axis=AX.X), vM)
                P.guard(f"bank{mb}", vM)
        P.guard("sq2", bK)
        P.wait("v", vM, vM.n)
        P.emit("v", lambda v: v.reduce_max(
            out=mh_s[0:1, 12:13], in_=mh_s[0:1, 0:4], axis=AX.X), vM)
        P.emit("v", lambda v: v.reduce_max(
            out=mh_s[0:1, 13:14], in_=mh_s[0:1, 4:12], axis=AX.X), vM)
        vfence()
        P.emit("v", lambda v: v.tensor_tensor(
            out=mh_s[0:1, 14:15], in0=mh_s[0:1, 12:13], in1=mh_s[0:1, 13:14],
            op=ALU.mult), vM)
        P.wait("s", vM, vM.n)
        P.use("s", "nm_r")
        P.emit("s", lambda s: s.activation(
            out=mh_s[0:1, 15:16], in_=mh_s[0:1, 14:15], func=AF.Ln), sM)
        sfence()
        P.emit("s", lambda s: s.activation(
            out=nm_r[:], in_=mh_s[0:1, 15:16], func=AF.Exp, scale=0.5), sM)
        P.wait("v", sM, sM.n)
        P.emit("v", lambda v: v.tensor_scalar(
            out=nm_r[:], in0=nm_r[:], scalar1=-0.125, scalar2=None, op0=ALU.mult), vM)
        vfence()
        P.emit("v", lambda v: v.tensor_copy(out=nm_rb[:], in_=nm_r[:]), vM)
        P.wait("t", vM, vM.n)
        P.use("t", "bank6")
        P.emit("t", lambda t: t.matmul(
            pb[6][:, 0:1], ones_r[:], nm_rb[:], start=True, stop=True), bK)
        P.wait("s", bK, bK.n)
        P.use("s", "nm_s")
        P.emit("s", lambda s: s.copy(out=nm_s[:], in_=pb[6][:, 0:1]), sM)
        P.guard("bank6", sM)
        nm_rt = sM.n

        exp_val = {}
        last_av = {}

        def emit_avpair(gkc):
            qc_, kc_ = gkc // KT, gkc % KT
            for h2 in range(2):
                avbank = 4 + (2 * qc_ + h2) % 4
                if kc_ == 0:
                    P.use("t", f"bank{avbank}")
                P.wait("t", fS, exp_val[gkc])
                P.emit("t", lambda t, avbank=avbank, h2=h2, kc_=kc_, gkc=gkc: t.matmul(
                    pb[avbank][0:65, :], V_p[:, kc_, h2 * 65:h2 * 65 + 65],
                    es[:, gkc % 4, h2 * 512:h2 * 512 + 512],
                    start=(kc_ == 0), stop=(kc_ == KT - 1)), avp)
            P.guard(("es", gkc % 4), avp)
            last_av[qc_] = avp.n

        def emit_norm(qc_):
            for h2 in range(2):
                avbank = 4 + (2 * qc_ + h2) % 4
                P.wait("v", avp, last_av[qc_])
                P.use("v", "den_s")
                P.emit("v", lambda v, avbank=avbank: v.reciprocal(
                    out=den_s[64:65, :], in_=pb[avbank][64:65, :]), vM)
                rslot = (2 * qc_ + h2) % 2
                rv = roundtrip(rslot, "den_s", den_s[64:65, :], rcpb[:, rslot, :], QW,
                               vM, vM.n, ("rcpb", rslot))
                P.wait("v", dR, rv)
                if h2 == 0:
                    P.emit("v", lambda v, avbank=avbank, hp=hp, qc_=qc_, rslot=rslot:
                           v.tensor_tensor(
                               out=attnT[0:64, hp, qc_ * 512:(qc_ + 1) * 512],
                               in0=pb[avbank][0:64, :], in1=rcpb[0:64, rslot, :],
                               op=ALU.mult), fAV)
                else:
                    P.use("v", "stg")
                    P.emit("v", lambda v, avbank=avbank, rslot=rslot: v.tensor_tensor(
                        out=stg[0:64, :], in0=pb[avbank][0:64, :],
                        in1=rcpb[0:64, rslot, :], op=ALU.mult), fAV)
                    P.wait("y", fAV, fAV.n)
                    P.wait("y", dR, dR.n)
                    P.emit("y", lambda y, hp=hp, qc_=qc_: y.dma_start(
                        out=attnT[64:128, hp, qc_ * 512:(qc_ + 1) * 512],
                        in_=stg[0:64, :]), dR, 16)
                    P.guard("stg", dR, dR.n)
                    P.wait("y", dR, dR.n)
                P.guard(f"bank{avbank}", fAV)
                P.guard(("rcpb", rslot), fAV)

        for gkc in range(QCN * KT):
            qc, kc = gkc // KT, gkc % KT
            b0_ = 2 * (gkc % 2)
            P.use("t", f"bank{b0_}")
            P.use("t", f"bank{b0_ + 1}")
            for h2 in range(2):
                hs = slice(h2 * 64, h2 * 64 + 64)
                P.emit("t", lambda t, hs=hs, kc=kc, qc=qc, h2=h2, gkc=gkc:
                       t.matmul(
                    pS2[gkc % 2][:, h2 * 512:h2 * 512 + 512],
                    K_p[hs, kc * 128:(kc + 1) * 128],
                    q_p[hs, qc * 512:(qc + 1) * 512], start=True, stop=True,
                    tile_position=(h2 * 64, 0)), bS if h2 == 1 else None)
            P.wait("s", bS, bS.n)
            if gkc == 0:
                P.wait("s", sM, nm_rt)
            P.use("s", ("es", gkc % 4))
            exp_val[gkc] = P.emit("s", lambda s, gkc=gkc: s.activation(
                out=es[:, gkc % 4, :], in_=pS2[gkc % 2][:], func=AF.Exp,
                scale=0.125, bias=nm_s[:]), fS)
            P.guard(f"bank{b0_}", fS)
            P.guard(f"bank{b0_ + 1}", fS)
            if gkc >= 3:
                emit_avpair(gkc - 3)
                if (gkc - 3) % KT == KT - 1:
                    emit_norm((gkc - 3) // KT)
        for gkc in range(QCN * KT - 3, QCN * KT):
            emit_avpair(gkc)
            if gkc % KT == KT - 1:
                emit_norm(gkc // KT)
        P.guard("nm_s", fS)
    attn_stg_dR = dR.n

    P.wait("t", dWm, misc_dW)

    def wo_stage(qc):
        p0, p1 = (4, 5) if qc % 2 == 0 else (6, 7)
        yp = qc % 2
        stat_pend = []

        def emit_stats(m, fy_v, sm_v):
            if m == 0:
                P.use("t", f"bank{p0}")
                P.use("t", f"bank{p1}")
            P.wait("t", fY, fy_v)
            P.emit("t", lambda t, m=m, p0=p0, yp=yp: t.matmul(
                pb[p0][0:1, :], ones_c[:], yres[:, yp, m, :],
                start=(m == 0), stop=(m == KC - 1)))
            P.wait("t", sM, sm_v)
            P.emit("t", lambda t, m=m, p1=p1: t.matmul(
                pb[p1][0:1, :], ones_c[:], y2f[:, m, :],
                start=(m == 0), stop=(m == KC - 1)), bST)
            P.guard(("y2f", m), bST)

        for m in range(KC):
            bank = m % 2
            P.use("t", f"bank{bank}")
            if qc == 0 and m == 0:
                P.wait("t", fAV, fAV.n)
                P.wait("t", dR, attn_stg_dR)
            for kc in range(KC):
                P.emit("t", lambda t, bank=bank, m=m, kc=kc, qc=qc: t.matmul(
                    pb[bank][:], wo_s[:, kc, m, :],
                    attnT[:, kc, qc * 512:(qc + 1) * 512],
                    start=(kc == 0), stop=(kc == KC - 1)),
                    bY if kc == KC - 1 else None)
            P.wait("v", bY, bY.n)
            P.use("v", ("yres", yp, m))
            P.emit("v", lambda v, bank=bank, m=m, qc=qc, yp=yp: v.scalar_tensor_tensor(
                out=yres[:, yp, m, :], in0=pb[bank][:], scalar=bo_s[:, m:m + 1],
                in1=xT[:, 0, m, qc * 512:(qc + 1) * 512],
                op0=ALU.add, op1=ALU.add), fY)
            P.guard(f"bank{bank}", fY)
            P.wait("s", fY, fY.n)
            P.use("s", ("y2f", m))
            P.emit("s", lambda s, m=m, yp=yp: s.activation(
                out=y2f[:, m, :], in_=yres[:, yp, m, :], func=AF.Square), sM)
            stat_pend.append((m, fY.n, sM.n))
            if len(stat_pend) >= 2:
                emit_stats(*stat_pend.pop(0))
        while stat_pend:
            emit_stats(*stat_pend.pop(0))
        return bST.n

    def ln1_stage(qc, bst_val):
        p0, p1 = (4, 5) if qc % 2 == 0 else (6, 7)
        yp = qc % 2
        P.wait("v", bST, bst_val)
        P.use("v", "rows")
        P.emit("v", lambda v, p0=p0: v.tensor_scalar(
            out=rowT[:], in0=pb[p0][0:1, :], scalar1=1.0 / D, scalar2=None,
            op0=ALU.mult), vM)
        P.emit("v", lambda v, p1=p1: v.tensor_scalar(
            out=rowA[:], in0=pb[p1][0:1, :], scalar1=1.0 / D, scalar2=None,
            op0=ALU.mult), vM)
        vfence()
        P.emit("v", lambda v: v.tensor_tensor(
            out=rowB[:], in0=rowT[:], in1=rowT[:], op=ALU.mult), vM)
        vfence()
        P.emit("v", lambda v: v.tensor_tensor(
            out=rowA[:], in0=rowA[:], in1=rowB[:], op=ALU.subtract), vM)
        P.guard(f"bank{p0}", vM)
        P.guard(f"bank{p1}", vM)
        P.wait("s", vM, vM.n)
        P.emit("s", lambda s: s.activation(
            out=rowA[:], in_=rowA[:], func=AF.Ln, bias=eps_s[0:1, :]), sM)
        sfence()
        P.emit("s", lambda s: s.activation(
            out=rowA[:], in_=rowA[:], func=AF.Exp, scale=-0.5), sM)
        P.wait("v", sM, sM.n)
        P.emit("v", lambda v: v.scalar_tensor_tensor(
            out=rowB[:], in0=rowT[:], scalar=-1.0, in1=rowA[:],
            op0=ALU.mult, op1=ALU.mult), vM)
        P.use("t", "bank2")
        P.use("t", "bank3")
        P.wait("t", sM, sM.n)
        P.wait("t", vM, vM.n)
        P.emit("t", lambda t: t.matmul(
            pb[2][:], ones32r[:], rowA[:], start=True, stop=True))
        P.emit("t", lambda t: t.matmul(
            pb[3][:], ones32r[:], rowB[:], start=True, stop=True), bK)
        P.guard("rows", bK, bK.n)
        P.wait("v", bK, bK.n)
        for m in range(KC):
            P.emit("v", lambda v, m=m, qc=qc, yp=yp: v.tensor_tensor(
                out=x1T[:, m, qc * 512:(qc + 1) * 512], in0=yres[:, yp, m, :],
                in1=pb[2][:], op=ALU.mult),
                vM if m == KC - 1 else None)
            P.guard(("yres", yp, m), vM, vM.n)
        vfence()
        for m in range(KC):
            P.emit("v", lambda v, m=m, qc=qc: v.tensor_tensor(
                out=x1T[:, m, qc * 512:(qc + 1) * 512],
                in0=x1T[:, m, qc * 512:(qc + 1) * 512],
                in1=pb[3][:], op=ALU.add), fX1)
        P.guard("bank2", fX1)
        P.guard("bank3", fX1)

    ln1_pend = None
    for qc in range(QCN if stop >= 3 else 0):
        bv_ = wo_stage(qc)
        if ln1_pend is not None:
            ln1_stage(*ln1_pend)
        ln1_pend = (qc, bv_)
    if ln1_pend is not None:
        ln1_stage(*ln1_pend)


    x1_done = fX1.n
    P.wait("t", fX1, x1_done)
    for qc in range(QCN if stop >= 4 else 0):
        P.use("t", "bank6")
        for kc in range(KC):
            P.emit("t", lambda t, kc=kc, qc=qc: t.matmul(
                pb[6][0:7, :], rw_s[:, kc, 0:7],
                x1T[:, kc, qc * 512:(qc + 1) * 512],
                start=(kc == 0), stop=(kc == KC - 1)),
                bK if kc == KC - 1 else None)
        P.wait("s", bK, bK.n)
        P.use("s", "lgs")
        P.emit("s", lambda s: s.copy(out=lgs[:], in_=pb[6][0:7, :]), sM)
        P.guard("bank6", sM)
        for ts in range(4):
            bank = ts % 2
            P.use("t", f"bank{bank}")
            P.wait("t", sM, sM.n)
            P.emit("t", lambda t, ts=ts, bank=bank: t.transpose(
                pT32[:, bank * 512:bank * 512 + 7],
                lgs[0:7, ts * 128:(ts + 1) * 128], id32[0:7, 0:7]), tA)
            P.wait("v", tA, tA.n)
            P.emit("v", lambda v, ts=ts, qc=qc, bank=bank: v.tensor_copy(
                out=lgt[:, qc * 4 + ts, 0:7], in_=pT32[:, bank * 512:bank * 512 + 7]), eT)
            P.guard(f"bank{bank}", eT)
        P.guard("lgs", eT)
    P.wait("s", eT, eT.n)
    for i in range(NT if stop >= 4 else 0):
        P.emit("s", lambda s, i=i: s.activation(
            out=prb[:, i, 0:7], in_=lgt[:, i, 0:7], func=AF.Exp,
            accum_out=scs[:, i, 7:8]), sM)
    P.wait("v", sM, sM.n)

    def vstep(build_fn):
        for i in range(NT if stop >= 4 else 0):
            P.emit("v", build_fn(i), vM if i == NT - 1 else None)
        vfence()

    vstep(lambda i: (lambda v, i=i: v.reciprocal(
        out=scs[:, i, 6:7], in_=scs[:, i, 7:8])))
    vstep(lambda i: (lambda v, i=i: v.tensor_scalar(
        out=prb[:, i, 0:7], in0=prb[:, i, 0:7], scalar1=scs[:, i, 6:7],
        scalar2=None, op0=ALU.mult)))
    vstep(lambda i: (lambda v, i=i: v.reduce_max(
        out=scs[:, i, 0:1], in_=prb[:, i, 0:7], axis=AX.X)))
    vstep(lambda i: (lambda v, i=i: v.tensor_scalar(
        out=msk[:, i, 0:7], in0=prb[:, i, 0:7], scalar1=scs[:, i, 0:1],
        scalar2=None, op0=ALU.is_ge)))
    vstep(lambda i: (lambda v, i=i: v.reduce_sum(
        out=scs[:, i, 1:2], in_=msk[:, i, 0:7], axis=AX.X)))
    vstep(lambda i: (lambda v, i=i: v.scalar_tensor_tensor(
        out=msk[:, i, 0:7], in0=msk[:, i, 0:7], scalar=-1e30, in1=prb[:, i, 0:7],
        op0=ALU.mult, op1=ALU.add)))
    vstep(lambda i: (lambda v, i=i: v.reduce_max(
        out=scs[:, i, 2:3], in_=msk[:, i, 0:7], axis=AX.X)))
    vstep(lambda i: (lambda v, i=i: v.tensor_scalar(
        out=scs[:, i, 3:4], in0=scs[:, i, 1:2], scalar1=1.5, scalar2=None,
        op0=ALU.is_ge)))
    vstep(lambda i: (lambda v, i=i: v.tensor_tensor(
        out=scs[:, i, 4:5], in0=scs[:, i, 0:1], in1=scs[:, i, 2:3], op=ALU.subtract)))
    vstep(lambda i: (lambda v, i=i: v.tensor_tensor(
        out=scs[:, i, 4:5], in0=scs[:, i, 4:5], in1=scs[:, i, 3:4], op=ALU.mult)))
    vstep(lambda i: (lambda v, i=i: v.tensor_tensor(
        out=scs[:, i, 4:5], in0=scs[:, i, 2:3], in1=scs[:, i, 4:5], op=ALU.add)))
    vstep(lambda i: (lambda v, i=i: v.tensor_scalar(
        out=msk[:, i, 0:7], in0=prb[:, i, 0:7], scalar1=scs[:, i, 4:5],
        scalar2=None, op0=ALU.is_ge)))
    vstep(lambda i: (lambda v, i=i: v.tensor_scalar(
        out=scs[:, i, 5:6], in0=scs[:, i, 0:1], scalar1=scs[:, i, 4:5],
        scalar2=1e-9, op0=ALU.add, op1=ALU.add)))
    vstep(lambda i: (lambda v, i=i: v.reciprocal(
        out=scs[:, i, 6:7], in_=scs[:, i, 5:6])))
    vstep(lambda i: (lambda v, i=i: v.tensor_tensor(
        out=comb[:, i, 0:7], in0=prb[:, i, 0:7], in1=msk[:, i, 0:7], op=ALU.mult)))
    vstep(lambda i: (lambda v, i=i: v.tensor_scalar(
        out=comb[:, i, 0:7], in0=comb[:, i, 0:7], scalar1=scs[:, i, 6:7],
        scalar2=None, op0=ALU.mult)))
    comb_vM = vM.n
    P.wait("t", vM, comb_vM)
    for tt in range(NT if stop >= 4 else 0):
        bank = tt % 2
        P.use("t", f"bank{bank}")
        P.emit("t", lambda t, tt=tt, bank=bank: t.transpose(
            pT32[0:7, bank * 512:bank * 512 + 128], comb[:, tt, 0:7], id32[:]), tA)
        P.wait("v", tA, tA.n)
        P.emit("v", lambda v, tt=tt, bank=bank: v.tensor_copy(
            out=combT[0:7, tt * 128:(tt + 1) * 128],
            in_=pT32[0:7, bank * 512:bank * 512 + 128]), eT)
        P.guard(f"bank{bank}", eT)
    if stop >= 4:
        P.wait("y", eT, eT.n)
        P.wait("y", dR, dR.n)
        P.emit("y", lambda y: y.dma_start(out=comb_d, in_=combT[0:7, :]), dR, 16)
        P.wait("y", dR, dR.n)
    comb_dR = dR.n

    for par in range(2):
        P.guards[("wg", par)] = (fX1, x1_done)
        P.guards[("wu", par)] = (fX1, x1_done)
        P.guards[("wd", par)] = (fX1, x1_done)
    pend = None
    hpar_ctr = 0

    def emit_down(par, hpar, fh_val, wd_val, e, fq, tc):
        P.wait("t", fH, fh_val)
        P.wait("t", dWd[par], wd_val)
        for dm in range(KC):
            ob = 4 + dm % 4
            P.use("t", f"bank{ob}")
            for fsub in range(4):
                P.emit("t", lambda t, ob=ob, par=par, hpar=hpar, fsub=fsub, dm=dm:
                       t.matmul(pb[ob][:],
                                wd[:, par, fsub, dm * 128:(dm + 1) * 128],
                                hbuf[:, hpar, fsub, :],
                                start=(fsub == 0), stop=(fsub == 3)),
                       bO if fsub == 3 else None)
            P.wait("v", bO, bO.n)
            P.use("v", ("accs", dm, tc))
            if e == 0 and fq == 0:
                P.emit("v", lambda v, ob=ob, dm=dm, tc=tc: v.tensor_copy(
                    out=acc[:, dm, tc * 512:(tc + 1) * 512], in_=pb[ob][:]), fO)
            else:
                P.emit("v", lambda v, ob=ob, dm=dm, tc=tc: v.tensor_tensor(
                    out=acc[:, dm, tc * 512:(tc + 1) * 512],
                    in0=acc[:, dm, tc * 512:(tc + 1) * 512], in1=pb[ob][:],
                    op=ALU.add), fO)
            P.guard(("accs", dm, tc), fO)
            P.guard(f"bank{ob}", fO)
        for fsub in range(4):
            P.guard(("hbuf", hpar, fsub), bO)
        P.guard(("wd", par), bO)

    for e in range(NEXP if stop >= 5 else 0):
        if e > 0:
            cbp = e % 2
            P.wait("g", dR, comb_dR)
            for tc in range(TCN):
                P.use("g", ("cb", cbp, tc))
                g_dma(cb_all[:, cbp, tc, :],
                      bass.AP(comb_d.tensor, (e - 1) * TOK + tc * 512, [[0, 128], [1, 512]]),
                      dWc)
            for tc in range(TCN):
                P.guards[("cb_ld", cbp, tc)] = (dWc, dWc.n)
        for fq in range(FQN):
            par = fq % 2
            P.use("g", ("wg", par))
            g_dma(wg[:, par, :, :], gt[e, :, :, fq * 512:(fq + 1) * 512], dWg[par])
            P.use("g", ("wu", par))
            g_dma(wu[:, par, :, :], ut[e, :, :, fq * 512:(fq + 1) * 512], dWu[par])
            P.use("g", ("wd", par))
            g_dma(wd[:, par, :, :], dtw[e, :, 4 * fq:4 * fq + 4, :], dWd[par])
            P.wait("t", dWg[par], dWg[par].n)
            P.wait("t", dWu[par], dWu[par].n)
            if e == 0 and fq == 0:
                P.wait("t", fX1, x1_done)
            for tc in range(TCN):
                hpar = hpar_ctr % 2
                hpar_ctr += 1
                for fsub in range(4):
                    gb, ub = fsub % 2, 2 + fsub % 2
                    P.use("t", f"bank{gb}")
                    for kc in range(KC):
                        P.emit("t", lambda t, gb=gb, par=par, kc=kc, fsub=fsub, tc=tc:
                               t.matmul(pb[gb][:],
                                        wg[:, par, kc, fsub * 128:(fsub + 1) * 128],
                                        x1T[:, kc, tc * 512:(tc + 1) * 512],
                                        start=(kc == 0), stop=(kc == KC - 1)),
                               bG if kc == KC - 1 else None)
                    P.use("t", f"bank{ub}")
                    for kc in range(KC):
                        P.emit("t", lambda t, ub=ub, par=par, kc=kc, fsub=fsub, tc=tc:
                               t.matmul(pb[ub][:],
                                        wu[:, par, kc, fsub * 128:(fsub + 1) * 128],
                                        x1T[:, kc, tc * 512:(tc + 1) * 512],
                                        start=(kc == 0), stop=(kc == KC - 1)),
                               bU if kc == KC - 1 else None)
                    P.wait("s", bG, bG.n)
                    P.use("s", ("s1b", fsub))
                    P.emit("s", lambda s, gb=gb, fsub=fsub: s.activation(
                        out=s1b[:, fsub, :], in_=pb[gb][:], func=AF.Silu), sS1)
                    P.guard(f"bank{gb}", sS1)
                    P.wait("v", bU, bU.n)
                    P.wait("v", sS1, sS1.n)
                    if e == 0:
                        P.use("v", ("hbuf", hpar, fsub))
                        P.emit("v", lambda v, ub=ub, hpar=hpar, fsub=fsub: v.tensor_tensor(
                            out=hbuf[:, hpar, fsub, :], in0=s1b[:, fsub, :],
                            in1=pb[ub][:], op=ALU.mult), fH)
                        P.guard(("s1b", fsub), fH)
                    else:
                        if fq == 0 and fsub == 0:
                            P.wait("v", dWc, P.guards[("cb_ld", e % 2, tc)][1])
                        P.use("v", ("t1b", fsub))
                        P.emit("v", lambda v, ub=ub, fsub=fsub: v.tensor_tensor(
                            out=t1b[:, fsub, :], in0=s1b[:, fsub, :],
                            in1=pb[ub][:], op=ALU.mult), fH)
                        P.guard(("s1b", fsub), fH)
                    P.guard(f"bank{ub}", fH)
                if pend is not None:
                    emit_down(*pend)
                if e > 0:
                    P.wait("v", fH, fH.n)
                    for fsub in range(4):
                        P.use("v", ("hbuf", hpar, fsub))
                        P.emit("v", lambda v, hpar=hpar, fsub=fsub, tc=tc, e=e:
                               v.tensor_tensor(
                            out=hbuf[:, hpar, fsub, :], in0=t1b[:, fsub, :],
                            in1=cb_all[:, e % 2, tc, :], op=ALU.mult), fH)
                        P.guard(("t1b", fsub), fH)
                pend = (par, hpar, fH.n, dWd[par].n, e, fq, tc)
            P.guard(("wg", par), bG)
            P.guard(("wu", par), bU)
        if e > 0:
            for tc in range(TCN):
                P.guard(("cb", e % 2, tc), fH)
    if pend is not None:
        emit_down(*pend)
    ffn_fO = fO.n
    otile_box = [0]

    def z_stage(qc):
        p0, p1 = (4, 5) if qc % 2 == 0 else (6, 7)
        zp = qc % 2
        P.wait("v", fO, ffn_fO)
        for m in range(KC):
            P.use("v", ("zbuf", zp, m))
            P.emit("v", lambda v, m=m, qc=qc, zp=zp: v.tensor_tensor(
                out=zbuf[:, zp, m, :], in0=acc[:, m, qc * 512:(qc + 1) * 512],
                in1=x1T[:, m, qc * 512:(qc + 1) * 512], op=ALU.add), vM)
            P.wait("s", vM, vM.n)
            P.use("s", ("z2f", m))
            P.emit("s", lambda s, m=m, zp=zp: s.activation(
                out=z2f[:, m, :], in_=zbuf[:, zp, m, :], func=AF.Square), sM)
            if m == 0:
                P.use("t", f"bank{p0}")
                P.use("t", f"bank{p1}")
            P.wait("t", vM, vM.n)
            P.emit("t", lambda t, m=m, p0=p0, zp=zp: t.matmul(
                pb[p0][0:1, :], ones_c[:], zbuf[:, zp, m, :],
                start=(m == 0), stop=(m == KC - 1)))
            P.wait("t", sM, sM.n)
            P.emit("t", lambda t, m=m, p1=p1: t.matmul(
                pb[p1][0:1, :], ones_c[:], z2f[:, m, :],
                start=(m == 0), stop=(m == KC - 1)), bST)
            P.guard(("z2f", m), bST)
        return bST.n

    def ln2_stage(qc, bst_val):
        p0, p1 = (4, 5) if qc % 2 == 0 else (6, 7)
        zp = qc % 2
        P.wait("v", bST, bst_val)
        P.use("v", "rows")
        P.emit("v", lambda v, p0=p0: v.tensor_scalar(
            out=rowT[:], in0=pb[p0][0:1, :], scalar1=1.0 / D, scalar2=None,
            op0=ALU.mult), vM)
        P.emit("v", lambda v, p1=p1: v.tensor_scalar(
            out=rowA[:], in0=pb[p1][0:1, :], scalar1=1.0 / D, scalar2=None,
            op0=ALU.mult), vM)
        vfence()
        P.emit("v", lambda v: v.tensor_tensor(
            out=rowB[:], in0=rowT[:], in1=rowT[:], op=ALU.mult), vM)
        vfence()
        P.emit("v", lambda v: v.tensor_tensor(
            out=rowA[:], in0=rowA[:], in1=rowB[:], op=ALU.subtract), vM)
        P.guard(f"bank{p0}", vM)
        P.guard(f"bank{p1}", vM)
        P.wait("s", vM, vM.n)
        P.emit("s", lambda s: s.activation(
            out=rowA[:], in_=rowA[:], func=AF.Ln, bias=eps_s[0:1, :]), sM)
        sfence()
        P.emit("s", lambda s: s.activation(
            out=rowA[:], in_=rowA[:], func=AF.Exp, scale=-0.5), sM)
        P.wait("v", sM, sM.n)
        P.emit("v", lambda v: v.scalar_tensor_tensor(
            out=rowB[:], in0=rowT[:], scalar=-1.0, in1=rowA[:],
            op0=ALU.mult, op1=ALU.mult), vM)
        P.use("t", "bank2")
        P.use("t", "bank3")
        P.wait("t", sM, sM.n)
        P.wait("t", vM, vM.n)
        P.emit("t", lambda t: t.matmul(
            pb[2][:], ones32r[:], rowA[:], start=True, stop=True))
        P.emit("t", lambda t: t.matmul(
            pb[3][:], ones32r[:], rowB[:], start=True, stop=True), bK)
        P.guard("rows", bK, bK.n)
        P.wait("v", bK, bK.n)
        P.use("v", "x2qc")
        for m in range(KC):
            P.emit("v", lambda v, m=m, zp=zp: v.tensor_tensor(
                out=x2qc[:, m, :], in0=zbuf[:, zp, m, :], in1=pb[2][:],
                op=ALU.mult), vM if m == KC - 1 else None)
            P.guard(("zbuf", zp, m), vM, vM.n)
        vfence()
        for m in range(KC):
            P.emit("v", lambda v, m=m: v.tensor_tensor(
                out=x2qc[:, m, :], in0=x2qc[:, m, :], in1=pb[3][:],
                op=ALU.add), vX2)
        P.guard("bank2", vX2)
        P.guard("bank3", vX2)
        P.wait("t", vX2, vX2.n)
        for ts in range(4):
            otile = otile_box[0]
            P.use("t", "bank0")
            P.use("t", "bank1")
            for m in range(KC):
                P.emit("t", lambda t, m=m, ts=ts: t.transpose(
                    pT32[:, m * 128:(m + 1) * 128],
                    x2qc[:, m, ts * 128:(ts + 1) * 128], id32[:]),
                    tA if m == KC - 1 else None)
            P.wait("s", tA, tA.n)
            P.use("s", ("outb", otile % 2))
            P.emit("s", lambda s, ot=otile: s.copy(
                out=outb[:, ot % 2, :], in_=pT32[:]), eT)
            P.guard("bank0", eT)
            P.guard("bank1", eT)
            P.wait("g", eT, eT.n)
            P.emit("g", lambda g, ot=otile: g.dma_start(
                out=out[ot * 128:(ot + 1) * 128, :], in_=outb[:, ot % 2, :]),
                dO[otile % 2], 16)
            P.guards[("outb", otile % 2)] = (dO[otile % 2], dO[otile % 2].n)
            otile_box[0] += 1
        P.guard("x2qc", eT)

    ln2_pend = None
    for qc in range(QCN if stop >= 6 else 0):
        bv_ = z_stage(qc)
        if ln2_pend is not None:
            ln2_stage(*ln2_pend)
        ln2_pend = (qc, bv_)
    if ln2_pend is not None:
        ln2_stage(*ln2_pend)


    P.wait("g", dO[0], dO[0].n)
    P.wait("g", dO[1], dO[1].n)
    P.wait("y", dO[0], dO[0].n)
    P.wait("y", dO[1], dO[1].n)

    with nc.Block() as block:
        @block.tensor
        def _(t):
            for fn in P.q["t"]:
                fn(t)

        @block.vector
        def _(v):
            for fn in P.q["v"]:
                fn(v)

        @block.scalar
        def _(s):
            for fn in P.q["s"]:
                fn(s)

        @block.gpsimd
        def _(g):
            for fn in P.q["g"]:
                fn(g)

        @block.sync
        def _(y):
            for fn in P.q["y"]:
                fn(y)

    return nc


def _pack_inputs(L0_attn_x, L0_attn_Wqkv, L0_attn_bqkv, L0_attn_Wo, L0_attn_bo,
                 L0_ffn_shared_up, L0_ffn_shared_gate, L0_ffn_shared_down,
                 L0_ffn_routed_up, L0_ffn_routed_gate, L0_ffn_routed_down,
                 L0_ffn_router_w):
    f32 = np.float32
    W = np.asarray(L0_attn_Wqkv, f32)
    bqkv = np.asarray(L0_attn_bqkv, f32)
    Wo = np.asarray(L0_attn_Wo, f32)
    bo = np.asarray(L0_attn_bo, f32)
    x = np.asarray(L0_attn_x, f32)

    hidx = np.arange(16)
    q_rows = (hidx[:, None] * 192 + np.arange(64)[None, :]).ravel()
    k_rows = q_rows + 64
    v_rows = q_rows + 128
    wqkv_t = np.empty((128, PAIRS, 3, KC, 128), f32)
    for j, rows in enumerate((q_rows, k_rows, v_rows)):
        Wt = W[rows].T.reshape(KC, 128, PAIRS, 128)
        wqkv_t[:, :, j, :, :] = Wt.transpose(1, 2, 0, 3)
    bq = bqkv[q_rows]; bk = bqkv[k_rows]; bv = bqkv[v_rows]
    bq_c = np.ascontiguousarray(bq.reshape(8, 128).T)
    bk_c = np.ascontiguousarray(bk.reshape(8, 128).T)
    bo_c = np.ascontiguousarray(bo.reshape(8, 128).T)
    bv_row = np.ascontiguousarray(bv.reshape(1, D))
    WoT = Wo.T.reshape(KC, 128, 8, 128).transpose(1, 0, 2, 3)
    wo_t = np.ascontiguousarray(WoT.reshape(128, KC * 8 * 128))

    G = np.concatenate([np.asarray(L0_ffn_shared_gate, f32)[None],
                        np.asarray(L0_ffn_routed_gate, f32)], 0)
    U = np.concatenate([np.asarray(L0_ffn_shared_up, f32)[None],
                        np.asarray(L0_ffn_routed_up, f32)], 0)
    Dn = np.concatenate([np.asarray(L0_ffn_shared_down, f32)[None],
                         np.asarray(L0_ffn_routed_down, f32)], 0)
    gt = np.ascontiguousarray(G.transpose(0, 2, 1).reshape(NEXP, KC, 128, DFF)
                              .transpose(0, 2, 1, 3))
    ut = np.ascontiguousarray(U.transpose(0, 2, 1).reshape(NEXP, KC, 128, DFF)
                              .transpose(0, 2, 1, 3))
    dtw = np.ascontiguousarray(Dn.transpose(0, 2, 1).reshape(NEXP, 32, 128, D)
                               .transpose(0, 2, 1, 3))
    rw = np.asarray(L0_ffn_router_w, f32)
    rw_t = np.ascontiguousarray(rw.T.reshape(KC, 128, 7).transpose(1, 0, 2))

    shared = dict(wqkv_t=wqkv_t, wo_t=wo_t, bq_c=bq_c, bk_c=bk_c, bo_c=bo_c,
                  bv_row=bv_row, ident32=np.eye(128, dtype=f32),
                  gt=gt, ut=ut, dtw=dtw, rw_t=rw_t)
    in_maps = []
    for c in range(8):
        b, h = c // 2, c % 2
        xb = x[b]
        if h == 0:
            xr = np.ascontiguousarray(xb)
        else:
            xr = np.ascontiguousarray(np.concatenate([xb[TOK:], xb[:TOK]], 0))
        in_maps.append(dict(shared, xkv=xr))
    return in_maps


def kernel(**inputs):
    in_maps = _pack_inputs(**inputs)
    nc = build_nc()
    res = run_bass_kernel_spmd(nc, in_maps, core_ids=list(range(8)))
    full = np.empty((B, N, D), np.float32)
    for c in range(8):
        full[c // 2, (c % 2) * TOK:(c % 2 + 1) * TOK] = res.results[c]["out"]
    return full


# version 14
# speedup vs baseline: 1.0177x; 1.0177x over previous
import numpy as np
import concourse.bass as bass
import concourse.mybir as mybir
from concourse.bass_utils import run_bass_kernel_spmd

F32 = mybir.dt.float32
BF16 = mybir.dt.bfloat16
AF = mybir.ActivationFunctionType
ALU = mybir.AluOpType
AX = mybir.AxisListType

B, N, D = 4, 4096, 1024
DFF, NEXP = 4096, 8
TOK = 2048
KC = 8
PAIRS = 8
KT = 32
QCN, QW = 4, 512
NT = 16
FQN = 8
TCN = 4


class Sem:
    def __init__(self, nc, name):
        self.h = nc.alloc_semaphore(name)
        self.n = 0

    def bump(self, k=1):
        self.n += k
        return self.n


class Plan:
    def __init__(self):
        self.q = {k: [] for k in ("t", "v", "s", "g", "y")}
        self.guards = {}
        self.last_wait = {k: {} for k in ("t", "v", "s", "g", "y")}

    def emit(self, eng, build, sem=None, k=1):
        if sem is None:
            self.q[eng].append(build)
            return None
        v = sem.bump(k)
        h = sem.h
        self.q[eng].append(lambda e, b=build, h=h, k=k: b(e).then_inc(h, k))
        return v

    def wait(self, eng, sem, val):
        if val and val > 0:
            lw = self.last_wait[eng]
            if lw.get(id(sem), -1) >= val:
                return
            lw[id(sem)] = val
            self.q[eng].append(lambda e, s=sem.h, v=val: e.wait_ge(s, v))

    def use(self, eng, res):
        g = self.guards.get(res)
        if g is not None:
            self.wait(eng, g[0], g[1])

    def guard(self, res, sem, val=None):
        self.guards[res] = (sem, sem.n if val is None else val)


def build_nc(stop=9):
    nc = bass.Bass("TRN2", target_bir_lowering=False)

    xkv = nc.dram_tensor("xkv", [N, D], F32, kind="ExternalInput").ap()
    wqkv_t = nc.dram_tensor("wqkv_t", [128, PAIRS, 3, KC, 128], F32, kind="ExternalInput").ap()
    wo_t = nc.dram_tensor("wo_t", [128, KC * 8 * 128], F32, kind="ExternalInput").ap()
    bq_c = nc.dram_tensor("bq_c", [128, 8], F32, kind="ExternalInput").ap()
    bk_c = nc.dram_tensor("bk_c", [128, 8], F32, kind="ExternalInput").ap()
    bo_c = nc.dram_tensor("bo_c", [128, 8], F32, kind="ExternalInput").ap()
    bv_row = nc.dram_tensor("bv_row", [1, D], F32, kind="ExternalInput").ap()
    ident32 = nc.dram_tensor("ident32", [128, 128], F32, kind="ExternalInput").ap()
    gt = nc.dram_tensor("gt", [NEXP, 128, KC, DFF], F32, kind="ExternalInput").ap()
    ut = nc.dram_tensor("ut", [NEXP, 128, KC, DFF], F32, kind="ExternalInput").ap()
    dtw = nc.dram_tensor("dtw", [NEXP, 128, 32, D], F32, kind="ExternalInput").ap()
    rw_t = nc.dram_tensor("rw_t", [128, KC, 7], F32, kind="ExternalInput").ap()
    out = nc.dram_tensor("out", [TOK, D], F32, kind="ExternalOutput").ap()
    rscr = nc.dram_tensor("rscr", [8, QW], F32, kind="Internal").ap()
    comb_d = nc.dram_tensor("comb_d", [7, TOK], BF16, kind="Internal").ap()

    al = nc.alloc_sbuf_tensor
    id32 = al("id32", [128, 128], F32)
    idbf = al("idbf", [128, 128], BF16)
    bq_s = al("bq_s", [128, 8], F32)
    bk_s = al("bk_s", [128, 8], F32)
    bo_s = al("bo_s", [128, 8], F32)
    bvr_s = al("bvr_s", [1, D], BF16)
    ones_c = al("ones_c", [128, 1], BF16)
    ones_r = al("ones_r", [1, 128], BF16)
    ones32r = al("ones32r", [1, 128], F32)
    rw_s = al("rw_s", [128, KC, 8], BF16)
    lgs = al("lgs", [7, QW], F32)
    lgt = al("lgt", [128, NT, 8], F32)
    prb = al("prb", [128, NT, 8], F32)
    msk = al("msk", [128, NT, 8], F32)
    scs = al("scs", [128, NT, 8], F32)
    comb = al("comb", [128, NT, 8], F32)
    combT = al("combT", [7, NT * 128], BF16)
    rowA = al("rowA", [1, QW], F32)
    rowB = al("rowB", [1, QW], F32)
    rowT = al("rowT", [1, QW], F32)
    colsA = al("colsA", [128, 4], F32)
    colsB = al("colsB", [128, 4], F32)
    nm_s = al("nm_s", [128, 1], F32)
    nm_r = al("nm_r", [1, 1], F32)
    nm_rb = al("nm_rb", [1, 1], BF16)
    mh_s = al("mh_s", [1, 16], F32)
    eps_s = al("eps_s", [128, 1], F32)
    padv = al("padv", [128, 8], F32)
    pads = al("pads", [128, 8], F32)
    abc_s = al("abc_s", [128, 2, QW], F32)
    stg = al("stg", [128, QW], BF16)

    a0 = nc.sbuf_base
    attnT = al("attnT", [128, PAIRS, TOK], BF16)
    xT = al("xT", [128, 2, KC, TOK], BF16)
    p0_end = nc.sbuf_base
    assert p0_end - a0 == 98304, (a0, p0_end)
    nc.sbuf_base = a0
    acc = al("acc", [128, KC, TOK], F32)
    nc.sbuf_base = a0 + 65536
    x1T = al("x1T", [128, KC, TOK], BF16)
    nc.sbuf_base = p0_end

    b0 = nc.sbuf_base
    K_p = al("K_p", [128, N], BF16)
    V_p = al("V_p", [128, KT, 130], BF16)
    q_p = al("q_p", [128, TOK], BF16)
    wqkv_s = al("wqkv_s", [128, 2, 3, KC, 128], BF16)
    es = al("es", [128, 4, 2 * QW], BF16)
    yres_off = nc.sbuf_base
    yres = al("yres", [128, 2, 8, QW], BF16)
    nc.sbuf_base = yres_off
    sq2 = al("sq2", [128, TOK], BF16)
    nc.sbuf_base = yres_off + 16384
    y2 = al("y2", [128, QW], BF16)
    den_s = al("den_s", [128, QW], F32)
    rcpb = al("rcpb", [128, 2, QW], F32)
    xload = al("xload", [128, 2, D], BF16)
    b_end = nc.sbuf_base
    nc.sbuf_base = b0
    y2f = al("y2f", [128, 8, 512], BF16)

    nc.sbuf_base = b0
    wg = al("wg", [128, 2, KC, 512], BF16)
    wu = al("wu", [128, 2, KC, 512], BF16)
    wd = al("wd", [128, 2, 4, D], BF16)
    cb_all = al("cb_all", [128, 2, TCN, 512], BF16)
    hbuf = al("hbuf", [128, 2, 4, 512], BF16)
    s1b = al("s1b", [128, 4, 512], BF16)
    t1b = al("t1b", [128, 4, 512], BF16)
    c_end = nc.sbuf_base
    nc.sbuf_base = b0
    zbuf = al("zbuf", [128, 2, 8, QW], BF16)
    z2f = al("z2f", [128, 8, QW], BF16)
    x2qc = al("x2qc", [128, 8, QW], F32)
    outb = al("outb", [128, 2, D], F32)
    d_end = nc.sbuf_base
    nc.sbuf_base = max(b_end, c_end, d_end)
    wo_s = al("wo_s", [128, KC, 8, 128], BF16)
    assert nc.sbuf_base <= nc.sbuf_top, (nc.sbuf_base, nc.sbuf_top)

    pz = nc.psum_base
    pb = [nc.alloc_psum_tensor(f"pb{i}", [128, 512], F32) for i in range(8)]
    nc.psum_base = pz
    pTb = nc.alloc_psum_tensor("pTb", [128, 4096], BF16)
    nc.psum_base = pz
    pT32 = nc.alloc_psum_tensor("pT32", [128, 1024], F32)
    nc.psum_base = pz
    pS2 = [nc.alloc_psum_tensor(f"pS2_{i}", [128, 1024], F32) for i in range(2)]
    nc.psum_base = pz + 8

    dX = [Sem(nc, "dX0"), Sem(nc, "dX1")]
    dWm = Sem(nc, "dWm"); dWc = Sem(nc, "dWc")
    dWq = [Sem(nc, "dWq0"), Sem(nc, "dWq1")]
    dWg = [Sem(nc, "dWg0"), Sem(nc, "dWg1")]
    dWu = [Sem(nc, "dWu0"), Sem(nc, "dWu1")]
    dWd = [Sem(nc, "dWd0"), Sem(nc, "dWd1")]
    dR = Sem(nc, "dR"); dO = [Sem(nc, "dO0"), Sem(nc, "dO1")]
    tA = Sem(nc, "tA"); eA = Sem(nc, "eA")
    bK = Sem(nc, "bK"); fK = Sem(nc, "fK")
    bS = Sem(nc, "bS"); fS = Sem(nc, "fS")
    avp = Sem(nc, "avp"); fAV = Sem(nc, "fAV")
    bY = Sem(nc, "bY"); fY = Sem(nc, "fY")
    bST = Sem(nc, "bST"); vM = Sem(nc, "vM"); sM = Sem(nc, "sM")
    fX1 = Sem(nc, "fX1")
    bG = Sem(nc, "bG"); bU = Sem(nc, "bU"); sS1 = Sem(nc, "sS1"); fH = Sem(nc, "fH")
    bO = Sem(nc, "bO"); fO = Sem(nc, "fO")
    vX2 = Sem(nc, "vX2"); eT = Sem(nc, "eT")

    P = Plan()

    def vfence():
        P.wait("v", vM, vM.n)

    def sfence():
        P.wait("s", sM, sM.n)

    def g_dma(dst, src, sem):
        return P.emit("g", lambda g, d=dst, s=src: g.dma_start(out=d, in_=s), sem, 16)

    P.emit("y", lambda y: y.dma_start(out=id32[:], in_=ident32), dR, 16)
    P.emit("y", lambda y: y.dma_start(out=bq_s[:], in_=bq_c), dR, 16)
    P.emit("y", lambda y: y.dma_start(out=bk_s[:], in_=bk_c), dR, 16)
    P.emit("y", lambda y: y.dma_start(out=bo_s[:], in_=bo_c), dR, 16)
    misc_dR = dR.n
    g_dma(idbf[:], ident32, dWm)
    g_dma(bvr_s[:], bv_row, dWm)
    g_dma(rw_s[:, :, 0:7], rw_t, dWm)
    g_dma(bass.AP(wo_s, 0, [[8192, 128], [1, 8192]]), wo_t, dWm)
    misc_dW = dWm.n
    P.emit("v", lambda v: v.memset(eps_s[:], 1e-5))
    P.emit("v", lambda v: v.memset(ones_c[:], 1.0))
    P.emit("v", lambda v: v.memset(ones_r[:], 1.0))
    P.emit("v", lambda v: v.memset(ones32r[:], 1.0))
    P.emit("v", lambda v: v.memset(V_p[:, :, 64:65], 1.0))
    P.emit("v", lambda v: v.memset(V_p[:, :, 129:130], 1.0), vM)
    misc_vM = vM.n

    if stop >= 1:
        phase_a_gate = True
    ntr = 0
    for tt in range(32 if stop >= 1 else 0):
        if tt >= 2:
            P.wait("g", eA, 8 * (tt - 1))
        P.emit("g", lambda g, tt=tt: g.dma_start(
            out=xload[:, tt % 2, :], in_=xkv[tt * 128:(tt + 1) * 128, :]), dX[tt % 2], 16)
        for kc in range(KC):
            n = ntr
            P.wait("t", dX[tt % 2], 16 * (tt // 2 + 1))
            if n == 0:
                P.wait("t", dR, misc_dR)
                P.wait("t", dWm, misc_dW)
            if n >= 4:
                P.wait("t", eA, n - 3)
            po = (n % 4) * 1024
            P.emit("t", lambda t, po=po, tt=tt, kc=kc: t.transpose(
                pTb[:, po:po + 128], xload[:, tt % 2, kc * 128:(kc + 1) * 128],
                idbf[:]), tA)
            P.wait("s", tA, n + 1)
            P.emit("s", lambda s, po=po, tt=tt, kc=kc: s.copy(
                out=xT[:, tt // 16, kc, (tt % 16) * 128:(tt % 16 + 1) * 128],
                in_=pTb[:, po:po + 128]), eA)
            ntr += 1
    P.guard("bank0", eA)
    P.guard("bank1", eA)
    P.guard("bank2", eA)
    P.guard("bank3", eA)

    def roundtrip(slot, src_key, src_row, dst_tile, width, src_sem, src_val, dst_key):
        P.wait("y", src_sem, src_val)
        P.wait("y", dR, dR.n)
        P.use("y", ("rslot", slot))
        P.emit("y", lambda y, sl=slot, sr=src_row, w=width: y.dma_start(
            out=bass.AP(rscr.tensor, sl * QW, [[QW, 1], [1, w]]), in_=sr), dR, 16)
        P.guards[src_key] = (dR, dR.n)
        P.wait("y", dR, dR.n)
        if dst_key is not None:
            P.use("y", dst_key)
        P.emit("y", lambda y, sl=slot, dd=dst_tile, w=width: y.dma_start(
            out=dd, in_=bass.AP(rscr.tensor, sl * QW, [[0, 128], [1, w]])), dR, 16)
        P.guard(("rslot", slot), dR, dR.n)
        P.wait("y", dR, dR.n)
        return dR.n

    for hp in range(PAIRS if stop >= 2 else 0):
        wpar = hp % 2
        P.use("g", ("wqkv_s", wpar))
        g_dma(wqkv_s[:, wpar], wqkv_t[:, hp, :, :, :], dWq[wpar])
        wq_ld = dWq[wpar].n
        if hp == 0:
            P.wait("t", dR, misc_dR)
            P.wait("t", dWm, misc_dW)
            P.wait("t", vM, misc_vM)
            P.wait("t", eA, 256)
        P.wait("t", dWq[wpar], wq_ld)

        for ncn in range(8):
            bank = ncn % 2
            P.use("t", f"bank{bank}")
            for kc in range(KC):
                P.emit("t", lambda t, bank=bank, ncn=ncn, kc=kc, wpar=wpar: t.matmul(
                    pb[bank][:], wqkv_s[:, wpar, 1, kc, :],
                    xT[:, ncn // 4, kc, (ncn % 4) * 512:(ncn % 4 + 1) * 512],
                    start=(kc == 0), stop=(kc == KC - 1)),
                    bK if kc == KC - 1 else None)
            P.wait("s", bK, bK.n)
            P.emit("s", lambda s, bank=bank, ncn=ncn, hp=hp: s.activation(
                out=K_p[:, ncn * 512:(ncn + 1) * 512], in_=pb[bank][:],
                func=AF.Identity, bias=bk_s[:, hp:hp + 1]), fK)
            P.guard(f"bank{bank}", fK)

        for kt in range(KT):
            bank = 2 + kt % 2
            P.use("t", f"bank{bank}")
            for kc in range(KC):
                P.emit("t", lambda t, bank=bank, kt=kt, kc=kc, wpar=wpar: t.matmul(
                    pb[bank][:, 0:128],
                    xT[:, kt // 16, kc, (kt % 16) * 128:(kt % 16 + 1) * 128],
                    wqkv_s[:, wpar, 2, kc, :], start=(kc == 0), stop=False))
            P.emit("t", lambda t, bank=bank, hp=hp: t.matmul(
                pb[bank][:, 0:128], ones_r[:], bvr_s[0:1, hp * 128:(hp + 1) * 128],
                start=False, stop=True), bK)
            P.wait("s", bK, bK.n)
            P.emit("s", lambda s, bank=bank, kt=kt: s.copy(
                out=bass.AP(V_p, kt * 130, [[KT * 130, 128], [65, 2], [1, 64]]),
                in_=pb[bank][:, 0:128]), fK)
            P.guard(f"bank{bank}", fK)

        for qc in range(QCN):
            bank = 4 + qc % 2
            P.use("t", f"bank{bank}")
            for kc in range(KC):
                P.emit("t", lambda t, bank=bank, qc=qc, kc=kc, wpar=wpar: t.matmul(
                    pb[bank][:], wqkv_s[:, wpar, 0, kc, :],
                    xT[:, 0, kc, qc * 512:(qc + 1) * 512],
                    start=(kc == 0), stop=(kc == KC - 1)),
                    bK if kc == KC - 1 else None)
            P.wait("s", bK, bK.n)
            P.emit("s", lambda s, bank=bank, qc=qc, hp=hp: s.activation(
                out=q_p[:, qc * 512:(qc + 1) * 512], in_=pb[bank][:],
                func=AF.Identity, bias=bq_s[:, hp:hp + 1]), fK)
            P.guard(f"bank{bank}", fK)
        P.guard(("wqkv_s", wpar), bK)

        P.wait("v", vM, vM.n)
        P.wait("s", fK, fK.n)
        P.use("s", "sq2")
        P.emit("s", lambda s: s.activation(out=sq2[:], in_=q_p[:], func=AF.Square), sM)
        P.wait("t", sM, sM.n)
        for ch in range(4):
            mb = 6 + ch % 2
            P.use("t", f"bank{mb}")
            P.emit("t", lambda t, ch=ch, mb=mb: t.matmul(
                pb[mb][0:1, :], ones_c[:], sq2[:, ch * 512:(ch + 1) * 512],
                start=True, stop=True), bK)
            P.wait("v", bK, bK.n)
            P.emit("v", lambda v, ch=ch, mb=mb: v.reduce_max(
                out=mh_s[0:1, ch:ch + 1], in_=pb[mb][0:1, :], axis=AX.X), vM)
            P.guard(f"bank{mb}", vM)
        for half in range(2):
            P.wait("s", bK, bK.n)
            P.emit("s", lambda s, half=half: s.activation(
                out=sq2[:], in_=K_p[:, half * 2048:(half + 1) * 2048],
                func=AF.Square), sM)
            P.wait("t", sM, sM.n)
            for ch in range(4):
                mb = 6 + ch % 2
                P.use("t", f"bank{mb}")
                P.emit("t", lambda t, ch=ch, mb=mb: t.matmul(
                    pb[mb][0:1, :], ones_c[:], sq2[:, ch * 512:(ch + 1) * 512],
                    start=True, stop=True), bK)
                P.wait("v", bK, bK.n)
                P.emit("v", lambda v, half=half, ch=ch, mb=mb: v.reduce_max(
                    out=mh_s[0:1, 4 + half * 4 + ch:5 + half * 4 + ch],
                    in_=pb[mb][0:1, :], axis=AX.X), vM)
                P.guard(f"bank{mb}", vM)
        P.guard("sq2", bK)
        P.wait("v", vM, vM.n)
        P.emit("v", lambda v: v.reduce_max(
            out=mh_s[0:1, 12:13], in_=mh_s[0:1, 0:4], axis=AX.X), vM)
        P.emit("v", lambda v: v.reduce_max(
            out=mh_s[0:1, 13:14], in_=mh_s[0:1, 4:12], axis=AX.X), vM)
        vfence()
        P.emit("v", lambda v: v.tensor_tensor(
            out=mh_s[0:1, 14:15], in0=mh_s[0:1, 12:13], in1=mh_s[0:1, 13:14],
            op=ALU.mult), vM)
        P.wait("s", vM, vM.n)
        P.use("s", "nm_r")
        P.emit("s", lambda s: s.activation(
            out=mh_s[0:1, 15:16], in_=mh_s[0:1, 14:15], func=AF.Ln), sM)
        sfence()
        P.emit("s", lambda s: s.activation(
            out=nm_r[:], in_=mh_s[0:1, 15:16], func=AF.Exp, scale=0.5), sM)
        P.wait("v", sM, sM.n)
        P.emit("v", lambda v: v.tensor_scalar(
            out=nm_r[:], in0=nm_r[:], scalar1=-0.125, scalar2=None, op0=ALU.mult), vM)
        vfence()
        P.emit("v", lambda v: v.tensor_copy(out=nm_rb[:], in_=nm_r[:]), vM)
        P.wait("t", vM, vM.n)
        P.use("t", "bank6")
        P.emit("t", lambda t: t.matmul(
            pb[6][:, 0:1], ones_r[:], nm_rb[:], start=True, stop=True), bK)
        P.wait("s", bK, bK.n)
        P.use("s", "nm_s")
        P.emit("s", lambda s: s.copy(out=nm_s[:], in_=pb[6][:, 0:1]), sM)
        P.guard("bank6", sM)
        nm_rt = sM.n

        exp_val = {}
        last_av = {}

        def emit_avpair(gkc):
            qc_, kc_ = gkc // KT, gkc % KT
            for h2 in range(2):
                avbank = 4 + (2 * qc_ + h2) % 4
                if kc_ == 0:
                    P.use("t", f"bank{avbank}")
                P.wait("t", fS, exp_val[gkc])
                P.emit("t", lambda t, avbank=avbank, h2=h2, kc_=kc_, gkc=gkc: t.matmul(
                    pb[avbank][0:65, :], V_p[:, kc_, h2 * 65:h2 * 65 + 65],
                    es[:, gkc % 4, h2 * 512:h2 * 512 + 512],
                    start=(kc_ == 0), stop=(kc_ == KT - 1)), avp)
            P.guard(("es", gkc % 4), avp)
            last_av[qc_] = avp.n

        def emit_norm(qc_):
            for h2 in range(2):
                avbank = 4 + (2 * qc_ + h2) % 4
                P.wait("v", avp, last_av[qc_])
                P.use("v", "den_s")
                P.emit("v", lambda v, avbank=avbank: v.reciprocal(
                    out=den_s[64:65, :], in_=pb[avbank][64:65, :]), vM)
                rslot = (2 * qc_ + h2) % 2
                rv = roundtrip(rslot, "den_s", den_s[64:65, :], rcpb[:, rslot, :], QW,
                               vM, vM.n, ("rcpb", rslot))
                P.wait("v", dR, rv)
                if h2 == 0:
                    P.emit("v", lambda v, avbank=avbank, hp=hp, qc_=qc_, rslot=rslot:
                           v.tensor_tensor(
                               out=attnT[0:64, hp, qc_ * 512:(qc_ + 1) * 512],
                               in0=pb[avbank][0:64, :], in1=rcpb[0:64, rslot, :],
                               op=ALU.mult), fAV)
                else:
                    P.use("v", "stg")
                    P.emit("v", lambda v, avbank=avbank, rslot=rslot: v.tensor_tensor(
                        out=stg[0:64, :], in0=pb[avbank][0:64, :],
                        in1=rcpb[0:64, rslot, :], op=ALU.mult), fAV)
                    P.wait("y", fAV, fAV.n)
                    P.wait("y", dR, dR.n)
                    P.emit("y", lambda y, hp=hp, qc_=qc_: y.dma_start(
                        out=attnT[64:128, hp, qc_ * 512:(qc_ + 1) * 512],
                        in_=stg[0:64, :]), dR, 16)
                    P.guard("stg", dR, dR.n)
                    P.wait("y", dR, dR.n)
                P.guard(f"bank{avbank}", fAV)
                P.guard(("rcpb", rslot), fAV)

        for gkc in range(QCN * KT):
            qc, kc = gkc // KT, gkc % KT
            b0_ = 2 * (gkc % 2)
            P.use("t", f"bank{b0_}")
            P.use("t", f"bank{b0_ + 1}")
            for h2 in range(2):
                hs = slice(h2 * 64, h2 * 64 + 64)
                P.emit("t", lambda t, hs=hs, kc=kc, qc=qc, h2=h2, gkc=gkc:
                       t.matmul(
                    pS2[gkc % 2][:, h2 * 512:h2 * 512 + 512],
                    K_p[hs, kc * 128:(kc + 1) * 128],
                    q_p[hs, qc * 512:(qc + 1) * 512], start=True, stop=True,
                    tile_position=(h2 * 64, 0)), bS if h2 == 1 else None)
            P.wait("s", bS, bS.n)
            if gkc == 0:
                P.wait("s", sM, nm_rt)
            P.use("s", ("es", gkc % 4))
            exp_val[gkc] = P.emit("s", lambda s, gkc=gkc: s.activation(
                out=es[:, gkc % 4, :], in_=pS2[gkc % 2][:], func=AF.Exp,
                scale=0.125, bias=nm_s[:]), fS)
            P.guard(f"bank{b0_}", fS)
            P.guard(f"bank{b0_ + 1}", fS)
            if gkc >= 3:
                emit_avpair(gkc - 3)
                if (gkc - 3) % KT == KT - 1:
                    emit_norm((gkc - 3) // KT)
        for gkc in range(QCN * KT - 3, QCN * KT):
            emit_avpair(gkc)
            if gkc % KT == KT - 1:
                emit_norm(gkc // KT)
        P.guard("nm_s", fS)
    attn_stg_dR = dR.n

    P.wait("t", dWm, misc_dW)

    def wo_stage(qc):
        p0, p1 = (4, 5) if qc % 2 == 0 else (6, 7)
        yp = qc % 2
        stat_pend = []

        def emit_stats(m, fy_v, sm_v):
            if m == 0:
                P.use("t", f"bank{p0}")
                P.use("t", f"bank{p1}")
            P.wait("t", fY, fy_v)
            P.emit("t", lambda t, m=m, p0=p0, yp=yp: t.matmul(
                pb[p0][0:1, :], ones_c[:], yres[:, yp, m, :],
                start=(m == 0), stop=(m == KC - 1)))
            P.wait("t", sM, sm_v)
            P.emit("t", lambda t, m=m, p1=p1: t.matmul(
                pb[p1][0:1, :], ones_c[:], y2f[:, m, :],
                start=(m == 0), stop=(m == KC - 1)), bST)
            P.guard(("y2f", m), bST)

        for m in range(KC):
            bank = m % 2
            P.use("t", f"bank{bank}")
            if qc == 0 and m == 0:
                P.wait("t", fAV, fAV.n)
                P.wait("t", dR, attn_stg_dR)
            for kc in range(KC):
                P.emit("t", lambda t, bank=bank, m=m, kc=kc, qc=qc: t.matmul(
                    pb[bank][:], wo_s[:, kc, m, :],
                    attnT[:, kc, qc * 512:(qc + 1) * 512],
                    start=(kc == 0), stop=(kc == KC - 1)),
                    bY if kc == KC - 1 else None)
            P.wait("v", bY, bY.n)
            P.use("v", ("yres", yp, m))
            P.emit("v", lambda v, bank=bank, m=m, qc=qc, yp=yp: v.scalar_tensor_tensor(
                out=yres[:, yp, m, :], in0=pb[bank][:], scalar=bo_s[:, m:m + 1],
                in1=xT[:, 0, m, qc * 512:(qc + 1) * 512],
                op0=ALU.add, op1=ALU.add), fY)
            P.guard(f"bank{bank}", fY)
            P.wait("s", fY, fY.n)
            P.use("s", ("y2f", m))
            P.emit("s", lambda s, m=m, yp=yp: s.activation(
                out=y2f[:, m, :], in_=yres[:, yp, m, :], func=AF.Square), sM)
            stat_pend.append((m, fY.n, sM.n))
            if len(stat_pend) >= 2:
                emit_stats(*stat_pend.pop(0))
        while stat_pend:
            emit_stats(*stat_pend.pop(0))
        return bST.n

    def ln1_stage(qc, bst_val):
        p0, p1 = (4, 5) if qc % 2 == 0 else (6, 7)
        yp = qc % 2
        P.wait("v", bST, bst_val)
        P.use("v", "rows")
        P.emit("v", lambda v, p0=p0: v.tensor_scalar(
            out=rowT[:], in0=pb[p0][0:1, :], scalar1=1.0 / D, scalar2=None,
            op0=ALU.mult), vM)
        P.emit("v", lambda v, p1=p1: v.tensor_scalar(
            out=rowA[:], in0=pb[p1][0:1, :], scalar1=1.0 / D, scalar2=None,
            op0=ALU.mult), vM)
        vfence()
        P.emit("v", lambda v: v.tensor_tensor(
            out=rowB[:], in0=rowT[:], in1=rowT[:], op=ALU.mult), vM)
        vfence()
        P.emit("v", lambda v: v.tensor_tensor(
            out=rowA[:], in0=rowA[:], in1=rowB[:], op=ALU.subtract), vM)
        P.guard(f"bank{p0}", vM)
        P.guard(f"bank{p1}", vM)
        P.wait("s", vM, vM.n)
        P.emit("s", lambda s: s.activation(
            out=rowA[:], in_=rowA[:], func=AF.Ln, bias=eps_s[0:1, :]), sM)
        sfence()
        P.emit("s", lambda s: s.activation(
            out=rowA[:], in_=rowA[:], func=AF.Exp, scale=-0.5), sM)
        P.wait("v", sM, sM.n)
        P.emit("v", lambda v: v.scalar_tensor_tensor(
            out=rowB[:], in0=rowT[:], scalar=-1.0, in1=rowA[:],
            op0=ALU.mult, op1=ALU.mult), vM)
        P.use("t", "bank2")
        P.use("t", "bank3")
        P.wait("t", sM, sM.n)
        P.wait("t", vM, vM.n)
        P.emit("t", lambda t: t.matmul(
            pb[2][:], ones32r[:], rowA[:], start=True, stop=True))
        P.emit("t", lambda t: t.matmul(
            pb[3][:], ones32r[:], rowB[:], start=True, stop=True), bK)
        P.guard("rows", bK, bK.n)
        P.wait("v", bK, bK.n)
        for m in range(KC):
            P.emit("v", lambda v, m=m, qc=qc, yp=yp: v.tensor_tensor(
                out=x1T[:, m, qc * 512:(qc + 1) * 512], in0=yres[:, yp, m, :],
                in1=pb[2][:], op=ALU.mult),
                vM if m == KC - 1 else None)
            P.guard(("yres", yp, m), vM, vM.n)
        vfence()
        for m in range(KC):
            P.emit("v", lambda v, m=m, qc=qc: v.tensor_tensor(
                out=x1T[:, m, qc * 512:(qc + 1) * 512],
                in0=x1T[:, m, qc * 512:(qc + 1) * 512],
                in1=pb[3][:], op=ALU.add), fX1)
        P.guard("bank2", fX1)
        P.guard("bank3", fX1)

    ln1_pend = None
    for qc in range(QCN if stop >= 3 else 0):
        bv_ = wo_stage(qc)
        if ln1_pend is not None:
            ln1_stage(*ln1_pend)
        ln1_pend = (qc, bv_)
    if ln1_pend is not None:
        ln1_stage(*ln1_pend)


    x1_done = fX1.n
    P.wait("t", fX1, x1_done)
    for qc in range(QCN if stop >= 4 else 0):
        P.use("t", "bank6")
        for kc in range(KC):
            P.emit("t", lambda t, kc=kc, qc=qc: t.matmul(
                pb[6][0:7, :], rw_s[:, kc, 0:7],
                x1T[:, kc, qc * 512:(qc + 1) * 512],
                start=(kc == 0), stop=(kc == KC - 1)),
                bK if kc == KC - 1 else None)
        P.wait("s", bK, bK.n)
        P.use("s", "lgs")
        P.emit("s", lambda s: s.copy(out=lgs[:], in_=pb[6][0:7, :]), sM)
        P.guard("bank6", sM)
        for ts in range(4):
            bank = ts % 2
            P.use("t", f"bank{bank}")
            P.wait("t", sM, sM.n)
            P.emit("t", lambda t, ts=ts, bank=bank: t.transpose(
                pT32[:, bank * 512:bank * 512 + 7],
                lgs[0:7, ts * 128:(ts + 1) * 128], id32[0:7, 0:7]), tA)
            P.wait("v", tA, tA.n)
            P.emit("v", lambda v, ts=ts, qc=qc, bank=bank: v.tensor_copy(
                out=lgt[:, qc * 4 + ts, 0:7], in_=pT32[:, bank * 512:bank * 512 + 7]), eT)
            P.guard(f"bank{bank}", eT)
        P.guard("lgs", eT)
    P.wait("s", eT, eT.n)
    for i in range(NT if stop >= 4 else 0):
        P.emit("s", lambda s, i=i: s.activation(
            out=prb[:, i, 0:7], in_=lgt[:, i, 0:7], func=AF.Exp,
            accum_out=scs[:, i, 7:8]), sM)
    P.wait("v", sM, sM.n)

    def vstep(build_fn):
        for i in range(NT if stop >= 4 else 0):
            P.emit("v", build_fn(i), vM if i == NT - 1 else None)
        vfence()

    vstep(lambda i: (lambda v, i=i: v.reciprocal(
        out=scs[:, i, 6:7], in_=scs[:, i, 7:8])))
    vstep(lambda i: (lambda v, i=i: v.tensor_scalar(
        out=prb[:, i, 0:7], in0=prb[:, i, 0:7], scalar1=scs[:, i, 6:7],
        scalar2=None, op0=ALU.mult)))
    vstep(lambda i: (lambda v, i=i: v.reduce_max(
        out=scs[:, i, 0:1], in_=prb[:, i, 0:7], axis=AX.X)))
    vstep(lambda i: (lambda v, i=i: v.tensor_scalar(
        out=msk[:, i, 0:7], in0=prb[:, i, 0:7], scalar1=scs[:, i, 0:1],
        scalar2=None, op0=ALU.is_ge)))
    vstep(lambda i: (lambda v, i=i: v.reduce_sum(
        out=scs[:, i, 1:2], in_=msk[:, i, 0:7], axis=AX.X)))
    vstep(lambda i: (lambda v, i=i: v.scalar_tensor_tensor(
        out=msk[:, i, 0:7], in0=msk[:, i, 0:7], scalar=-1e30, in1=prb[:, i, 0:7],
        op0=ALU.mult, op1=ALU.add)))
    vstep(lambda i: (lambda v, i=i: v.reduce_max(
        out=scs[:, i, 2:3], in_=msk[:, i, 0:7], axis=AX.X)))
    vstep(lambda i: (lambda v, i=i: v.tensor_scalar(
        out=scs[:, i, 3:4], in0=scs[:, i, 1:2], scalar1=1.5, scalar2=None,
        op0=ALU.is_ge)))
    vstep(lambda i: (lambda v, i=i: v.tensor_tensor(
        out=scs[:, i, 4:5], in0=scs[:, i, 0:1], in1=scs[:, i, 2:3], op=ALU.subtract)))
    vstep(lambda i: (lambda v, i=i: v.tensor_tensor(
        out=scs[:, i, 4:5], in0=scs[:, i, 4:5], in1=scs[:, i, 3:4], op=ALU.mult)))
    vstep(lambda i: (lambda v, i=i: v.tensor_tensor(
        out=scs[:, i, 4:5], in0=scs[:, i, 2:3], in1=scs[:, i, 4:5], op=ALU.add)))
    vstep(lambda i: (lambda v, i=i: v.tensor_scalar(
        out=msk[:, i, 0:7], in0=prb[:, i, 0:7], scalar1=scs[:, i, 4:5],
        scalar2=None, op0=ALU.is_ge)))
    vstep(lambda i: (lambda v, i=i: v.tensor_scalar(
        out=scs[:, i, 5:6], in0=scs[:, i, 0:1], scalar1=scs[:, i, 4:5],
        scalar2=1e-9, op0=ALU.add, op1=ALU.add)))
    vstep(lambda i: (lambda v, i=i: v.reciprocal(
        out=scs[:, i, 6:7], in_=scs[:, i, 5:6])))
    vstep(lambda i: (lambda v, i=i: v.tensor_tensor(
        out=comb[:, i, 0:7], in0=prb[:, i, 0:7], in1=msk[:, i, 0:7], op=ALU.mult)))
    vstep(lambda i: (lambda v, i=i: v.tensor_scalar(
        out=comb[:, i, 0:7], in0=comb[:, i, 0:7], scalar1=scs[:, i, 6:7],
        scalar2=None, op0=ALU.mult)))
    comb_vM = vM.n
    P.wait("t", vM, comb_vM)
    for tt in range(NT if stop >= 4 else 0):
        bank = tt % 2
        P.use("t", f"bank{bank}")
        P.emit("t", lambda t, tt=tt, bank=bank: t.transpose(
            pT32[0:7, bank * 512:bank * 512 + 128], comb[:, tt, 0:7], id32[:]), tA)
        P.wait("v", tA, tA.n)
        P.emit("v", lambda v, tt=tt, bank=bank: v.tensor_copy(
            out=combT[0:7, tt * 128:(tt + 1) * 128],
            in_=pT32[0:7, bank * 512:bank * 512 + 128]), eT)
        P.guard(f"bank{bank}", eT)
    if stop >= 4:
        P.wait("y", eT, eT.n)
        P.wait("y", dR, dR.n)
        P.emit("y", lambda y: y.dma_start(out=comb_d, in_=combT[0:7, :]), dR, 16)
        P.wait("y", dR, dR.n)
    comb_dR = dR.n

    for par in range(2):
        P.guards[("wg", par)] = (fX1, x1_done)
        P.guards[("wu", par)] = (fX1, x1_done)
        P.guards[("wd", par)] = (fX1, x1_done)
    pend = None
    hpar_ctr = 0

    def emit_down(par, hpar, fh_val, wd_val, e, fq, tc):
        P.wait("t", fH, fh_val)
        P.wait("t", dWd[par], wd_val)
        for dm in range(KC):
            ob = 4 + dm % 4
            P.use("t", f"bank{ob}")
            for fsub in range(4):
                P.emit("t", lambda t, ob=ob, par=par, hpar=hpar, fsub=fsub, dm=dm:
                       t.matmul(pb[ob][:],
                                wd[:, par, fsub, dm * 128:(dm + 1) * 128],
                                hbuf[:, hpar, fsub, :],
                                start=(fsub == 0), stop=(fsub == 3)),
                       bO if fsub == 3 else None)
            P.wait("v", bO, bO.n)
            P.use("v", ("accs", dm, tc))
            if e == 0 and fq == 0:
                P.emit("v", lambda v, ob=ob, dm=dm, tc=tc: v.tensor_copy(
                    out=acc[:, dm, tc * 512:(tc + 1) * 512], in_=pb[ob][:]), fO)
            else:
                P.emit("v", lambda v, ob=ob, dm=dm, tc=tc: v.tensor_tensor(
                    out=acc[:, dm, tc * 512:(tc + 1) * 512],
                    in0=acc[:, dm, tc * 512:(tc + 1) * 512], in1=pb[ob][:],
                    op=ALU.add), fO)
            P.guard(("accs", dm, tc), fO)
            P.guard(f"bank{ob}", fO)
        for fsub in range(4):
            P.guard(("hbuf", hpar, fsub), bO)
        P.guard(("wd", par), bO)

    for e in range(NEXP if stop >= 5 else 0):
        if e > 0:
            cbp = e % 2
            P.wait("g", dR, comb_dR)
            for tc in range(TCN):
                P.use("g", ("cb", cbp, tc))
                g_dma(cb_all[:, cbp, tc, :],
                      bass.AP(comb_d.tensor, (e - 1) * TOK + tc * 512, [[0, 128], [1, 512]]),
                      dWc)
            for tc in range(TCN):
                P.guards[("cb_ld", cbp, tc)] = (dWc, dWc.n)
        for fq in range(FQN):
            par = fq % 2
            P.use("g", ("wg", par))
            g_dma(wg[:, par, :, :], gt[e, :, :, fq * 512:(fq + 1) * 512], dWg[par])
            P.use("g", ("wu", par))
            g_dma(wu[:, par, :, :], ut[e, :, :, fq * 512:(fq + 1) * 512], dWu[par])
            P.use("g", ("wd", par))
            g_dma(wd[:, par, :, :], dtw[e, :, 4 * fq:4 * fq + 4, :], dWd[par])
            P.wait("t", dWg[par], dWg[par].n)
            P.wait("t", dWu[par], dWu[par].n)
            if e == 0 and fq == 0:
                P.wait("t", fX1, x1_done)
            for tc in range(TCN):
                hpar = hpar_ctr % 2
                hpar_ctr += 1
                for fsub in range(4):
                    gb, ub = fsub % 2, 2 + fsub % 2
                    P.use("t", f"bank{gb}")
                    for kc in range(KC):
                        P.emit("t", lambda t, gb=gb, par=par, kc=kc, fsub=fsub, tc=tc:
                               t.matmul(pb[gb][:],
                                        wg[:, par, kc, fsub * 128:(fsub + 1) * 128],
                                        x1T[:, kc, tc * 512:(tc + 1) * 512],
                                        start=(kc == 0), stop=(kc == KC - 1)),
                               bG if kc == KC - 1 else None)
                    P.use("t", f"bank{ub}")
                    for kc in range(KC):
                        P.emit("t", lambda t, ub=ub, par=par, kc=kc, fsub=fsub, tc=tc:
                               t.matmul(pb[ub][:],
                                        wu[:, par, kc, fsub * 128:(fsub + 1) * 128],
                                        x1T[:, kc, tc * 512:(tc + 1) * 512],
                                        start=(kc == 0), stop=(kc == KC - 1)),
                               bU if kc == KC - 1 else None)
                    P.wait("s", bG, bG.n)
                    P.use("s", ("s1b", fsub))
                    P.emit("s", lambda s, gb=gb, fsub=fsub: s.activation(
                        out=s1b[:, fsub, :], in_=pb[gb][:], func=AF.Silu), sS1)
                    P.guard(f"bank{gb}", sS1)
                    P.wait("v", bU, bU.n)
                    P.wait("v", sS1, sS1.n)
                    if e == 0:
                        P.use("v", ("hbuf", hpar, fsub))
                        P.emit("v", lambda v, ub=ub, hpar=hpar, fsub=fsub: v.tensor_tensor(
                            out=hbuf[:, hpar, fsub, :], in0=s1b[:, fsub, :],
                            in1=pb[ub][:], op=ALU.mult), fH)
                        P.guard(("s1b", fsub), fH)
                    else:
                        if fq == 0 and fsub == 0:
                            P.wait("v", dWc, P.guards[("cb_ld", e % 2, tc)][1])
                        P.use("v", ("t1b", fsub))
                        P.emit("v", lambda v, ub=ub, fsub=fsub: v.tensor_tensor(
                            out=t1b[:, fsub, :], in0=s1b[:, fsub, :],
                            in1=pb[ub][:], op=ALU.mult), fH)
                        P.guard(("s1b", fsub), fH)
                    P.guard(f"bank{ub}", fH)
                if pend is not None:
                    emit_down(*pend)
                if e > 0:
                    P.wait("v", fH, fH.n)
                    for fsub in range(4):
                        P.use("v", ("hbuf", hpar, fsub))
                        P.emit("v", lambda v, hpar=hpar, fsub=fsub, tc=tc, e=e:
                               v.tensor_tensor(
                            out=hbuf[:, hpar, fsub, :], in0=t1b[:, fsub, :],
                            in1=cb_all[:, e % 2, tc, :], op=ALU.mult), fH)
                        P.guard(("t1b", fsub), fH)
                pend = (par, hpar, fH.n, dWd[par].n, e, fq, tc)
            P.guard(("wg", par), bG)
            P.guard(("wu", par), bU)
        if e > 0:
            for tc in range(TCN):
                P.guard(("cb", e % 2, tc), fH)
    if pend is not None:
        emit_down(*pend)
    ffn_fO = fO.n
    otile_box = [0]

    def z_stage(qc):
        p0, p1 = (4, 5) if qc % 2 == 0 else (6, 7)
        zp = qc % 2
        P.wait("v", fO, ffn_fO)
        for m in range(KC):
            P.use("v", ("zbuf", zp, m))
            P.emit("v", lambda v, m=m, qc=qc, zp=zp: v.tensor_tensor(
                out=zbuf[:, zp, m, :], in0=acc[:, m, qc * 512:(qc + 1) * 512],
                in1=x1T[:, m, qc * 512:(qc + 1) * 512], op=ALU.add), vM)
            P.wait("s", vM, vM.n)
            P.use("s", ("z2f", m))
            P.emit("s", lambda s, m=m, zp=zp: s.activation(
                out=z2f[:, m, :], in_=zbuf[:, zp, m, :], func=AF.Square), sM)
            if m == 0:
                P.use("t", f"bank{p0}")
                P.use("t", f"bank{p1}")
            P.wait("t", vM, vM.n)
            P.emit("t", lambda t, m=m, p0=p0, zp=zp: t.matmul(
                pb[p0][0:1, :], ones_c[:], zbuf[:, zp, m, :],
                start=(m == 0), stop=(m == KC - 1)))
            P.wait("t", sM, sM.n)
            P.emit("t", lambda t, m=m, p1=p1: t.matmul(
                pb[p1][0:1, :], ones_c[:], z2f[:, m, :],
                start=(m == 0), stop=(m == KC - 1)), bST)
            P.guard(("z2f", m), bST)
        return bST.n

    def ln2_stage(qc, bst_val):
        p0, p1 = (4, 5) if qc % 2 == 0 else (6, 7)
        zp = qc % 2
        P.wait("v", bST, bst_val)
        P.use("v", "rows")
        P.emit("v", lambda v, p0=p0: v.tensor_scalar(
            out=rowT[:], in0=pb[p0][0:1, :], scalar1=1.0 / D, scalar2=None,
            op0=ALU.mult), vM)
        P.emit("v", lambda v, p1=p1: v.tensor_scalar(
            out=rowA[:], in0=pb[p1][0:1, :], scalar1=1.0 / D, scalar2=None,
            op0=ALU.mult), vM)
        vfence()
        P.emit("v", lambda v: v.tensor_tensor(
            out=rowB[:], in0=rowT[:], in1=rowT[:], op=ALU.mult), vM)
        vfence()
        P.emit("v", lambda v: v.tensor_tensor(
            out=rowA[:], in0=rowA[:], in1=rowB[:], op=ALU.subtract), vM)
        P.guard(f"bank{p0}", vM)
        P.guard(f"bank{p1}", vM)
        P.wait("s", vM, vM.n)
        P.emit("s", lambda s: s.activation(
            out=rowA[:], in_=rowA[:], func=AF.Ln, bias=eps_s[0:1, :]), sM)
        sfence()
        P.emit("s", lambda s: s.activation(
            out=rowA[:], in_=rowA[:], func=AF.Exp, scale=-0.5), sM)
        P.wait("v", sM, sM.n)
        P.emit("v", lambda v: v.scalar_tensor_tensor(
            out=rowB[:], in0=rowT[:], scalar=-1.0, in1=rowA[:],
            op0=ALU.mult, op1=ALU.mult), vM)
        P.use("t", "bank2")
        P.use("t", "bank3")
        P.wait("t", sM, sM.n)
        P.wait("t", vM, vM.n)
        for ts in range(4):
            P.emit("t", lambda t, ts=ts: t.transpose(
                pb[2][:, ts:ts + 1], rowA[0:1, ts * 128:(ts + 1) * 128],
                id32[0:1, 0:1]))
            P.emit("t", lambda t, ts=ts: t.transpose(
                pb[3][:, ts:ts + 1], rowB[0:1, ts * 128:(ts + 1) * 128],
                id32[0:1, 0:1]), bK if ts == 3 else None)
        P.guard("rows", bK, bK.n)
        P.wait("v", bK, bK.n)
        P.use("v", "cols")
        P.emit("v", lambda v: v.tensor_copy(out=colsA[:], in_=pb[2][:, 0:4]), vM)
        P.emit("v", lambda v: v.tensor_copy(out=colsB[:], in_=pb[3][:, 0:4]), vM)
        P.guard("bank2", vM)
        P.guard("bank3", vM)
        P.wait("s", vM, vM.n)
        for ts in range(4):
            otile = otile_box[0]
            tp = ts % 2
            P.use("t", f"bank{tp}")
            for m in range(KC):
                P.emit("t", lambda t, m=m, ts=ts, zp=zp, tp=tp: t.transpose(
                    pTb[:, tp * 1024 + m * 128:tp * 1024 + (m + 1) * 128],
                    zbuf[:, zp, m, ts * 128:(ts + 1) * 128], idbf[:]),
                    tA if m == KC - 1 else None)
            P.wait("s", tA, tA.n)
            P.use("s", ("outb", otile % 2))
            P.emit("s", lambda s, ot=otile, ts=ts, tp=tp: s.activation(
                out=outb[:, ot % 2, :], in_=pTb[:, tp * 1024:(tp + 1) * 1024],
                func=AF.Identity, scale=colsA[:, ts:ts + 1],
                bias=colsB[:, ts:ts + 1]), eT)
            P.guard(f"bank{tp}", eT)
            P.wait("g", eT, eT.n)
            P.emit("g", lambda g, ot=otile: g.dma_start(
                out=out[ot * 128:(ot + 1) * 128, :], in_=outb[:, ot % 2, :]),
                dO[otile % 2], 16)
            P.guards[("outb", otile % 2)] = (dO[otile % 2], dO[otile % 2].n)
            otile_box[0] += 1
        for m in range(KC):
            P.guard(("zbuf", zp, m), tA)
        P.guard("cols", eT)

    ln2_pend = None
    for qc in range(QCN if stop >= 6 else 0):
        bv_ = z_stage(qc)
        if ln2_pend is not None:
            ln2_stage(*ln2_pend)
        ln2_pend = (qc, bv_)
    if ln2_pend is not None:
        ln2_stage(*ln2_pend)


    P.wait("g", dO[0], dO[0].n)
    P.wait("g", dO[1], dO[1].n)
    P.wait("y", dO[0], dO[0].n)
    P.wait("y", dO[1], dO[1].n)

    with nc.Block() as block:
        @block.tensor
        def _(t):
            for fn in P.q["t"]:
                fn(t)

        @block.vector
        def _(v):
            for fn in P.q["v"]:
                fn(v)

        @block.scalar
        def _(s):
            for fn in P.q["s"]:
                fn(s)

        @block.gpsimd
        def _(g):
            for fn in P.q["g"]:
                fn(g)

        @block.sync
        def _(y):
            for fn in P.q["y"]:
                fn(y)

    return nc


def _pack_inputs(L0_attn_x, L0_attn_Wqkv, L0_attn_bqkv, L0_attn_Wo, L0_attn_bo,
                 L0_ffn_shared_up, L0_ffn_shared_gate, L0_ffn_shared_down,
                 L0_ffn_routed_up, L0_ffn_routed_gate, L0_ffn_routed_down,
                 L0_ffn_router_w):
    f32 = np.float32
    W = np.asarray(L0_attn_Wqkv, f32)
    bqkv = np.asarray(L0_attn_bqkv, f32)
    Wo = np.asarray(L0_attn_Wo, f32)
    bo = np.asarray(L0_attn_bo, f32)
    x = np.asarray(L0_attn_x, f32)

    hidx = np.arange(16)
    q_rows = (hidx[:, None] * 192 + np.arange(64)[None, :]).ravel()
    k_rows = q_rows + 64
    v_rows = q_rows + 128
    wqkv_t = np.empty((128, PAIRS, 3, KC, 128), f32)
    for j, rows in enumerate((q_rows, k_rows, v_rows)):
        Wt = W[rows].T.reshape(KC, 128, PAIRS, 128)
        wqkv_t[:, :, j, :, :] = Wt.transpose(1, 2, 0, 3)
    bq = bqkv[q_rows]; bk = bqkv[k_rows]; bv = bqkv[v_rows]
    bq_c = np.ascontiguousarray(bq.reshape(8, 128).T)
    bk_c = np.ascontiguousarray(bk.reshape(8, 128).T)
    bo_c = np.ascontiguousarray(bo.reshape(8, 128).T)
    bv_row = np.ascontiguousarray(bv.reshape(1, D))
    WoT = Wo.T.reshape(KC, 128, 8, 128).transpose(1, 0, 2, 3)
    wo_t = np.ascontiguousarray(WoT.reshape(128, KC * 8 * 128))

    G = np.concatenate([np.asarray(L0_ffn_shared_gate, f32)[None],
                        np.asarray(L0_ffn_routed_gate, f32)], 0)
    U = np.concatenate([np.asarray(L0_ffn_shared_up, f32)[None],
                        np.asarray(L0_ffn_routed_up, f32)], 0)
    Dn = np.concatenate([np.asarray(L0_ffn_shared_down, f32)[None],
                         np.asarray(L0_ffn_routed_down, f32)], 0)
    gt = np.ascontiguousarray(G.transpose(0, 2, 1).reshape(NEXP, KC, 128, DFF)
                              .transpose(0, 2, 1, 3))
    ut = np.ascontiguousarray(U.transpose(0, 2, 1).reshape(NEXP, KC, 128, DFF)
                              .transpose(0, 2, 1, 3))
    dtw = np.ascontiguousarray(Dn.transpose(0, 2, 1).reshape(NEXP, 32, 128, D)
                               .transpose(0, 2, 1, 3))
    rw = np.asarray(L0_ffn_router_w, f32)
    rw_t = np.ascontiguousarray(rw.T.reshape(KC, 128, 7).transpose(1, 0, 2))

    shared = dict(wqkv_t=wqkv_t, wo_t=wo_t, bq_c=bq_c, bk_c=bk_c, bo_c=bo_c,
                  bv_row=bv_row, ident32=np.eye(128, dtype=f32),
                  gt=gt, ut=ut, dtw=dtw, rw_t=rw_t)
    in_maps = []
    for c in range(8):
        b, h = c // 2, c % 2
        xb = x[b]
        if h == 0:
            xr = np.ascontiguousarray(xb)
        else:
            xr = np.ascontiguousarray(np.concatenate([xb[TOK:], xb[:TOK]], 0))
        in_maps.append(dict(shared, xkv=xr))
    return in_maps


def kernel(**inputs):
    in_maps = _pack_inputs(**inputs)
    nc = build_nc()
    res = run_bass_kernel_spmd(nc, in_maps, core_ids=list(range(8)))
    full = np.empty((B, N, D), np.float32)
    for c in range(8):
        full[c // 2, (c % 2) * TOK:(c % 2 + 1) * TOK] = res.results[c]["out"]
    return full


# version 15 (final)
# speedup vs baseline: 1.1202x; 1.1007x over previous
import numpy as np
import concourse.bass as bass
import concourse.mybir as mybir
from concourse.bass_utils import run_bass_kernel_spmd

F32 = mybir.dt.float32
BF16 = mybir.dt.bfloat16
AF = mybir.ActivationFunctionType
ALU = mybir.AluOpType
AX = mybir.AxisListType

B, N, D = 4, 4096, 1024
DFF, NEXP = 4096, 8
TOK = 2048
KC = 8
PAIRS = 8
KT = 32
QCN, QW = 4, 512
NT = 16
FQN = 8
TCN = 4


class Sem:
    def __init__(self, nc, name):
        self.h = nc.alloc_semaphore(name)
        self.n = 0

    def bump(self, k=1):
        self.n += k
        return self.n


class Plan:
    def __init__(self):
        self.q = {k: [] for k in ("t", "v", "s", "g", "y")}
        self.guards = {}
        self.last_wait = {k: {} for k in ("t", "v", "s", "g", "y")}

    def emit(self, eng, build, sem=None, k=1):
        if sem is None:
            self.q[eng].append(build)
            return None
        v = sem.bump(k)
        h = sem.h
        self.q[eng].append(lambda e, b=build, h=h, k=k: b(e).then_inc(h, k))
        return v

    def wait(self, eng, sem, val):
        if val and val > 0:
            lw = self.last_wait[eng]
            if lw.get(id(sem), -1) >= val:
                return
            lw[id(sem)] = val
            self.q[eng].append(lambda e, s=sem.h, v=val: e.wait_ge(s, v))

    def use(self, eng, res):
        g = self.guards.get(res)
        if g is not None:
            self.wait(eng, g[0], g[1])

    def guard(self, res, sem, val=None):
        self.guards[res] = (sem, sem.n if val is None else val)


def build_nc(stop=9):
    nc = bass.Bass("TRN2", target_bir_lowering=False)

    xkv = nc.dram_tensor("xkv", [N, D], F32, kind="ExternalInput").ap()
    wqkv_t = nc.dram_tensor("wqkv_t", [128, PAIRS, 3, KC, 128], F32, kind="ExternalInput").ap()
    wo_t = nc.dram_tensor("wo_t", [128, KC * 8 * 128], F32, kind="ExternalInput").ap()
    bq_c = nc.dram_tensor("bq_c", [128, 8], F32, kind="ExternalInput").ap()
    bk_c = nc.dram_tensor("bk_c", [128, 8], F32, kind="ExternalInput").ap()
    bo_c = nc.dram_tensor("bo_c", [128, 8], F32, kind="ExternalInput").ap()
    bv_row = nc.dram_tensor("bv_row", [1, D], F32, kind="ExternalInput").ap()
    ident32 = nc.dram_tensor("ident32", [128, 128], F32, kind="ExternalInput").ap()
    gt = nc.dram_tensor("gt", [NEXP, 128, KC, DFF], F32, kind="ExternalInput").ap()
    ut = nc.dram_tensor("ut", [NEXP, 128, KC, DFF], F32, kind="ExternalInput").ap()
    dtw = nc.dram_tensor("dtw", [NEXP, 128, 32, D], F32, kind="ExternalInput").ap()
    rw_t = nc.dram_tensor("rw_t", [128, KC, 7], F32, kind="ExternalInput").ap()
    out = nc.dram_tensor("out", [TOK, D], F32, kind="ExternalOutput").ap()
    rscr = nc.dram_tensor("rscr", [8, QW], F32, kind="Internal").ap()
    comb_d = nc.dram_tensor("comb_d", [7, TOK], BF16, kind="Internal").ap()

    al = nc.alloc_sbuf_tensor
    id32 = al("id32", [128, 128], F32)
    idbf = al("idbf", [128, 128], BF16)
    bq_s = al("bq_s", [128, 8], F32)
    bk_s = al("bk_s", [128, 8], F32)
    bo_s = al("bo_s", [128, 8], F32)
    bvr_s = al("bvr_s", [1, D], BF16)
    ones_c = al("ones_c", [128, 1], BF16)
    ones_r = al("ones_r", [1, 128], BF16)
    ones32r = al("ones32r", [1, 128], F32)
    rw_s = al("rw_s", [128, KC, 8], BF16)
    lgs = al("lgs", [7, QW], F32)
    lgt = al("lgt", [128, NT, 8], F32)
    prb = al("prb", [128, NT, 8], F32)
    msk = al("msk", [128, NT, 8], F32)
    scs = al("scs", [128, NT, 8], F32)
    comb = al("comb", [128, NT, 8], F32)
    combT = al("combT", [7, NT * 128], BF16)
    rowA = al("rowA", [1, QW], F32)
    rowB = al("rowB", [1, QW], F32)
    rowT = al("rowT", [1, QW], F32)
    colsA = al("colsA", [128, 4], F32)
    colsB = al("colsB", [128, 4], F32)
    nm_s = al("nm_s", [128, 1], F32)
    nm_r = al("nm_r", [1, 1], F32)
    nm_rb = al("nm_rb", [1, 1], BF16)
    mh_s = al("mh_s", [1, 16], F32)
    eps_s = al("eps_s", [128, 1], F32)
    padv = al("padv", [128, 8], F32)
    pads = al("pads", [128, 8], F32)
    abc_s = al("abc_s", [128, 2, QW], F32)
    stg = al("stg", [128, QW], BF16)

    a0 = nc.sbuf_base
    attnT = al("attnT", [128, PAIRS, TOK], BF16)
    xT = al("xT", [128, 2, KC, TOK], BF16)
    p0_end = nc.sbuf_base
    assert p0_end - a0 == 98304, (a0, p0_end)
    nc.sbuf_base = a0
    acc = al("acc", [128, KC, TOK], F32)
    nc.sbuf_base = a0 + 65536
    x1T = al("x1T", [128, KC, TOK], BF16)
    nc.sbuf_base = p0_end

    b0 = nc.sbuf_base
    K_p = al("K_p", [128, N], BF16)
    V_p = al("V_p", [128, KT, 130], BF16)
    q_p = al("q_p", [128, TOK], BF16)
    wqkv_s = al("wqkv_s", [128, 2, 3, KC, 128], BF16)
    es = al("es", [128, 4, 2 * QW], BF16)
    yres_off = nc.sbuf_base
    yres = al("yres", [128, 2, 8, QW], BF16)
    nc.sbuf_base = yres_off
    sq2 = al("sq2", [128, TOK], BF16)
    nc.sbuf_base = yres_off + 16384
    y2 = al("y2", [128, QW], BF16)
    den_s = al("den_s", [128, QW], F32)
    rcpb = al("rcpb", [128, 2, QW], F32)
    xload = al("xload", [128, 2, D], BF16)
    b_end = nc.sbuf_base
    nc.sbuf_base = b0
    y2f = al("y2f", [128, 8, 512], BF16)

    nc.sbuf_base = b0
    wg = al("wg", [128, 2, KC, 512], BF16)
    wu = al("wu", [128, 2, KC, 512], BF16)
    wd = al("wd", [128, 2, 4, D], BF16)
    cb_all = al("cb_all", [128, 2, TCN, 512], BF16)
    hbuf = al("hbuf", [128, 2, 4, 512], BF16)
    s1b = al("s1b", [128, 4, 512], BF16)
    t1b = al("t1b", [128, 4, 512], BF16)
    c_end = nc.sbuf_base
    nc.sbuf_base = b0
    zbuf = al("zbuf", [128, 2, 8, QW], BF16)
    z2f = al("z2f", [128, 8, QW], BF16)
    x2qc = al("x2qc", [128, 8, QW], F32)
    outb = al("outb", [128, 2, D], F32)
    d_end = nc.sbuf_base
    nc.sbuf_base = max(b_end, c_end, d_end)
    wo_s = al("wo_s", [128, KC, 8, 128], BF16)
    assert nc.sbuf_base <= nc.sbuf_top, (nc.sbuf_base, nc.sbuf_top)

    pz = nc.psum_base
    pb = [nc.alloc_psum_tensor(f"pb{i}", [128, 512], F32) for i in range(8)]
    nc.psum_base = pz
    pTb = nc.alloc_psum_tensor("pTb", [128, 4096], BF16)
    nc.psum_base = pz
    pT32 = nc.alloc_psum_tensor("pT32", [128, 1024], F32)
    nc.psum_base = pz
    pS2 = [nc.alloc_psum_tensor(f"pS2_{i}", [128, 1024], F32) for i in range(2)]
    nc.psum_base = pz + 8

    dX = [Sem(nc, "dX0"), Sem(nc, "dX1")]
    dWm = Sem(nc, "dWm"); dWc = Sem(nc, "dWc")
    dWq = [Sem(nc, "dWq0"), Sem(nc, "dWq1")]
    dWg = [Sem(nc, "dWg0"), Sem(nc, "dWg1")]
    dWu = [Sem(nc, "dWu0"), Sem(nc, "dWu1")]
    dWd = [Sem(nc, "dWd0"), Sem(nc, "dWd1")]
    dR = Sem(nc, "dR"); dO = [Sem(nc, "dO0"), Sem(nc, "dO1")]
    tA = Sem(nc, "tA"); eA = Sem(nc, "eA")
    bK = Sem(nc, "bK"); fK = Sem(nc, "fK")
    bS = Sem(nc, "bS"); fS = Sem(nc, "fS")
    avp = Sem(nc, "avp"); fAV = Sem(nc, "fAV")
    bY = Sem(nc, "bY"); fY = Sem(nc, "fY")
    bST = Sem(nc, "bST"); vM = Sem(nc, "vM"); sM = Sem(nc, "sM")
    fX1 = Sem(nc, "fX1")
    bG = Sem(nc, "bG"); bU = Sem(nc, "bU"); sS1 = Sem(nc, "sS1"); fH = Sem(nc, "fH")
    bO = Sem(nc, "bO"); fO = Sem(nc, "fO")
    vX2 = Sem(nc, "vX2"); eT = Sem(nc, "eT")

    P = Plan()

    def vfence():
        P.wait("v", vM, vM.n)

    def sfence():
        P.wait("s", sM, sM.n)

    def g_dma(dst, src, sem):
        return P.emit("g", lambda g, d=dst, s=src: g.dma_start(out=d, in_=s), sem, 16)

    P.emit("y", lambda y: y.dma_start(out=id32[:], in_=ident32), dR, 16)
    P.emit("y", lambda y: y.dma_start(out=bq_s[:], in_=bq_c), dR, 16)
    P.emit("y", lambda y: y.dma_start(out=bk_s[:], in_=bk_c), dR, 16)
    P.emit("y", lambda y: y.dma_start(out=bo_s[:], in_=bo_c), dR, 16)
    misc_dR = dR.n
    g_dma(idbf[:], ident32, dWm)
    g_dma(bvr_s[:], bv_row, dWm)
    g_dma(rw_s[:, :, 0:7], rw_t, dWm)
    g_dma(bass.AP(wo_s, 0, [[8192, 128], [1, 8192]]), wo_t, dWm)
    misc_dW = dWm.n
    P.emit("v", lambda v: v.memset(eps_s[:], 1e-5))
    P.emit("v", lambda v: v.memset(ones_c[:], 1.0))
    P.emit("v", lambda v: v.memset(ones_r[:], 1.0))
    P.emit("v", lambda v: v.memset(ones32r[:], 1.0))
    P.emit("v", lambda v: v.memset(V_p[:, :, 64:65], 1.0))
    P.emit("v", lambda v: v.memset(V_p[:, :, 129:130], 1.0), vM)
    misc_vM = vM.n

    if stop >= 1:
        phase_a_gate = True
    ntr = 0
    for tt in range(32 if stop >= 1 else 0):
        if tt >= 2:
            P.wait("g", eA, 8 * (tt - 1))
        P.emit("g", lambda g, tt=tt: g.dma_start(
            out=xload[:, tt % 2, :], in_=xkv[tt * 128:(tt + 1) * 128, :]), dX[tt % 2], 16)
        for kc in range(KC):
            n = ntr
            P.wait("t", dX[tt % 2], 16 * (tt // 2 + 1))
            if n == 0:
                P.wait("t", dR, misc_dR)
                P.wait("t", dWm, misc_dW)
            if n >= 4:
                P.wait("t", eA, n - 3)
            po = (n % 4) * 1024
            P.emit("t", lambda t, po=po, tt=tt, kc=kc: t.transpose(
                pTb[:, po:po + 128], xload[:, tt % 2, kc * 128:(kc + 1) * 128],
                idbf[:]), tA)
            P.wait("s", tA, n + 1)
            P.emit("s", lambda s, po=po, tt=tt, kc=kc: s.copy(
                out=xT[:, tt // 16, kc, (tt % 16) * 128:(tt % 16 + 1) * 128],
                in_=pTb[:, po:po + 128]), eA)
            ntr += 1
    P.guard("bank0", eA)
    P.guard("bank1", eA)
    P.guard("bank2", eA)
    P.guard("bank3", eA)

    def roundtrip(slot, src_key, src_row, dst_tile, width, src_sem, src_val, dst_key):
        P.wait("y", src_sem, src_val)
        P.wait("y", dR, dR.n)
        P.use("y", ("rslot", slot))
        P.emit("y", lambda y, sl=slot, sr=src_row, w=width: y.dma_start(
            out=bass.AP(rscr.tensor, sl * QW, [[QW, 1], [1, w]]), in_=sr), dR, 16)
        P.guards[src_key] = (dR, dR.n)
        P.wait("y", dR, dR.n)
        if dst_key is not None:
            P.use("y", dst_key)
        P.emit("y", lambda y, sl=slot, dd=dst_tile, w=width: y.dma_start(
            out=dd, in_=bass.AP(rscr.tensor, sl * QW, [[0, 128], [1, w]])), dR, 16)
        P.guard(("rslot", slot), dR, dR.n)
        P.wait("y", dR, dR.n)
        return dR.n

    for hp in range(PAIRS if stop >= 2 else 0):
        wpar = hp % 2
        P.use("g", ("wqkv_s", wpar))
        g_dma(wqkv_s[:, wpar], wqkv_t[:, hp, :, :, :], dWq[wpar])
        wq_ld = dWq[wpar].n
        if hp == 0:
            P.wait("t", dR, misc_dR)
            P.wait("t", dWm, misc_dW)
            P.wait("t", vM, misc_vM)
            P.wait("t", eA, 256)
        P.wait("t", dWq[wpar], wq_ld)

        for kt in range(KT):
            bank = 2 + kt % 2
            P.use("t", f"bank{bank}")
            for kc in range(KC):
                P.emit("t", lambda t, bank=bank, kt=kt, kc=kc, wpar=wpar: t.matmul(
                    pb[bank][:, 0:128],
                    xT[:, kt // 16, kc, (kt % 16) * 128:(kt % 16 + 1) * 128],
                    wqkv_s[:, wpar, 2, kc, :], start=(kc == 0), stop=False))
            P.emit("t", lambda t, bank=bank, hp=hp: t.matmul(
                pb[bank][:, 0:128], ones_r[:], bvr_s[0:1, hp * 128:(hp + 1) * 128],
                start=False, stop=True), bK)
            P.wait("s", bK, bK.n)
            P.emit("s", lambda s, bank=bank, kt=kt: s.copy(
                out=bass.AP(V_p, kt * 130, [[KT * 130, 128], [65, 2], [1, 64]]),
                in_=pb[bank][:, 0:128]), fK)
            P.guard(f"bank{bank}", fK)

        P.wait("v", vM, vM.n)
        P.use("s", "sq2")
        stat_pend = [None]

        def emit_stat(idx, mb, sl, sm_v):
            P.use("t", f"bank{mb}")
            P.wait("t", sM, sm_v)
            P.emit("t", lambda t, mb=mb, sl=sl: t.matmul(
                pb[mb][0:1, :], ones_c[:], sq2[:, sl * 512:(sl + 1) * 512],
                start=True, stop=True), bK)
            P.guard(("sq2sl", sl), bK)
            P.wait("v", bK, bK.n)
            P.emit("v", lambda v, idx=idx, mb=mb: v.reduce_max(
                out=mh_s[0:1, idx:idx + 1], in_=pb[mb][0:1, :], axis=AX.X), vM)
            P.guard(f"bank{mb}", vM)

        for qc in range(QCN):
            bank = 4 + qc % 2
            P.use("t", f"bank{bank}")
            for kc in range(KC):
                P.emit("t", lambda t, bank=bank, qc=qc, kc=kc, wpar=wpar: t.matmul(
                    pb[bank][:], wqkv_s[:, wpar, 0, kc, :],
                    xT[:, 0, kc, qc * 512:(qc + 1) * 512],
                    start=(kc == 0), stop=(kc == KC - 1)),
                    bK if kc == KC - 1 else None)
            if stat_pend[0] is not None:
                emit_stat(*stat_pend[0])
            P.wait("s", bK, bK.n)
            P.emit("s", lambda s, bank=bank, qc=qc, hp=hp: s.activation(
                out=q_p[:, qc * 512:(qc + 1) * 512], in_=pb[bank][:],
                func=AF.Identity, bias=bq_s[:, hp:hp + 1]), fK)
            P.guard(f"bank{bank}", fK)
            P.wait("s", fK, fK.n)
            P.use("s", ("sq2sl", qc % 4))
            P.emit("s", lambda s, qc=qc: s.activation(
                out=sq2[:, (qc % 4) * 512:(qc % 4 + 1) * 512],
                in_=q_p[:, qc * 512:(qc + 1) * 512], func=AF.Square), sM)
            stat_pend[0] = (qc, 6 + qc % 2, qc % 4, sM.n)

        for ncn in range(8):
            bank = ncn % 2
            P.use("t", f"bank{bank}")
            for kc in range(KC):
                P.emit("t", lambda t, bank=bank, ncn=ncn, kc=kc, wpar=wpar: t.matmul(
                    pb[bank][:], wqkv_s[:, wpar, 1, kc, :],
                    xT[:, ncn // 4, kc, (ncn % 4) * 512:(ncn % 4 + 1) * 512],
                    start=(kc == 0), stop=(kc == KC - 1)),
                    bK if kc == KC - 1 else None)
            if stat_pend[0] is not None:
                emit_stat(*stat_pend[0])
            P.wait("s", bK, bK.n)
            P.emit("s", lambda s, bank=bank, ncn=ncn, hp=hp: s.activation(
                out=K_p[:, ncn * 512:(ncn + 1) * 512], in_=pb[bank][:],
                func=AF.Identity, bias=bk_s[:, hp:hp + 1]), fK)
            P.guard(f"bank{bank}", fK)
            P.wait("s", fK, fK.n)
            P.use("s", ("sq2sl", ncn % 4))
            P.emit("s", lambda s, ncn=ncn: s.activation(
                out=sq2[:, (ncn % 4) * 512:(ncn % 4 + 1) * 512],
                in_=K_p[:, ncn * 512:(ncn + 1) * 512], func=AF.Square), sM)
            stat_pend[0] = (4 + ncn, 6 + ncn % 2, ncn % 4, sM.n)
        emit_stat(*stat_pend[0])
        P.guard("sq2", bK)
        P.guard(("wqkv_s", wpar), bK)

        P.wait("v", vM, vM.n)
        P.emit("v", lambda v: v.reduce_max(
            out=mh_s[0:1, 12:13], in_=mh_s[0:1, 0:4], axis=AX.X), vM)
        P.emit("v", lambda v: v.reduce_max(
            out=mh_s[0:1, 13:14], in_=mh_s[0:1, 4:12], axis=AX.X), vM)
        vfence()
        P.emit("v", lambda v: v.tensor_tensor(
            out=mh_s[0:1, 14:15], in0=mh_s[0:1, 12:13], in1=mh_s[0:1, 13:14],
            op=ALU.mult), vM)
        P.wait("s", vM, vM.n)
        P.use("s", "nm_r")
        P.emit("s", lambda s: s.activation(
            out=mh_s[0:1, 15:16], in_=mh_s[0:1, 14:15], func=AF.Ln), sM)
        sfence()
        P.emit("s", lambda s: s.activation(
            out=nm_r[:], in_=mh_s[0:1, 15:16], func=AF.Exp, scale=0.5), sM)
        P.wait("v", sM, sM.n)
        P.emit("v", lambda v: v.tensor_scalar(
            out=nm_r[:], in0=nm_r[:], scalar1=-0.125, scalar2=None, op0=ALU.mult), vM)
        vfence()
        P.emit("v", lambda v: v.tensor_copy(out=nm_rb[:], in_=nm_r[:]), vM)
        nm_vM = vM.n

        exp_val = {}
        last_av = {}

        def emit_avpair(gkc):
            qc_, kc_ = gkc // KT, gkc % KT
            for h2 in range(2):
                avbank = 4 + (2 * qc_ + h2) % 4
                if kc_ == 0:
                    P.use("t", f"bank{avbank}")
                P.wait("t", fS, exp_val[gkc])
                P.emit("t", lambda t, avbank=avbank, h2=h2, kc_=kc_, gkc=gkc: t.matmul(
                    pb[avbank][0:65, :], V_p[:, kc_, h2 * 65:h2 * 65 + 65],
                    es[:, gkc % 4, h2 * 512:h2 * 512 + 512],
                    start=(kc_ == 0), stop=(kc_ == KT - 1)), avp)
            P.guard(("es", gkc % 4), avp)
            last_av[qc_] = avp.n

        def emit_norm(qc_):
            for h2 in range(2):
                avbank = 4 + (2 * qc_ + h2) % 4
                P.wait("v", avp, last_av[qc_])
                P.use("v", "den_s")
                P.emit("v", lambda v, avbank=avbank: v.reciprocal(
                    out=den_s[64:65, :], in_=pb[avbank][64:65, :]), vM)
                rslot = (2 * qc_ + h2) % 2
                rv = roundtrip(rslot, "den_s", den_s[64:65, :], rcpb[:, rslot, :], QW,
                               vM, vM.n, ("rcpb", rslot))
                P.wait("v", dR, rv)
                if h2 == 0:
                    P.emit("v", lambda v, avbank=avbank, hp=hp, qc_=qc_, rslot=rslot:
                           v.tensor_tensor(
                               out=attnT[0:64, hp, qc_ * 512:(qc_ + 1) * 512],
                               in0=pb[avbank][0:64, :], in1=rcpb[0:64, rslot, :],
                               op=ALU.mult), fAV)
                else:
                    P.use("v", "stg")
                    P.emit("v", lambda v, avbank=avbank, rslot=rslot: v.tensor_tensor(
                        out=stg[0:64, :], in0=pb[avbank][0:64, :],
                        in1=rcpb[0:64, rslot, :], op=ALU.mult), fAV)
                    P.wait("y", fAV, fAV.n)
                    P.wait("y", dR, dR.n)
                    P.emit("y", lambda y, hp=hp, qc_=qc_: y.dma_start(
                        out=attnT[64:128, hp, qc_ * 512:(qc_ + 1) * 512],
                        in_=stg[0:64, :]), dR, 16)
                    P.guard("stg", dR, dR.n)
                    P.wait("y", dR, dR.n)
                P.guard(f"bank{avbank}", fAV)
                P.guard(("rcpb", rslot), fAV)

        sc_val = {}

        def emit_scores(gkc):
            qc_, kc_ = gkc // KT, gkc % KT
            b0_ = 2 * (gkc % 2)
            P.use("t", f"bank{b0_}")
            P.use("t", f"bank{b0_ + 1}")
            for h2 in range(2):
                hs = slice(h2 * 64, h2 * 64 + 64)
                P.emit("t", lambda t, hs=hs, kc=kc_, qc=qc_, h2=h2, gkc=gkc:
                       t.matmul(
                    pS2[gkc % 2][:, h2 * 512:h2 * 512 + 512],
                    K_p[hs, kc * 128:(kc + 1) * 128],
                    q_p[hs, qc * 512:(qc + 1) * 512], start=True, stop=True,
                    tile_position=(h2 * 64, 0)), bS if h2 == 1 else None)
            sc_val[gkc] = bS.n

        emit_scores(0)
        P.wait("t", vM, nm_vM)
        P.use("t", "bank6")
        P.emit("t", lambda t: t.matmul(
            pb[6][:, 0:1], ones_r[:], nm_rb[:], start=True, stop=True), bK)
        P.wait("s", bK, bK.n)
        P.use("s", "nm_s")
        P.emit("s", lambda s: s.copy(out=nm_s[:], in_=pb[6][:, 0:1]), sM)
        P.guard("bank6", sM)
        nm_rt = sM.n

        for gkc in range(QCN * KT):
            qc, kc = gkc // KT, gkc % KT
            b0_ = 2 * (gkc % 2)
            if gkc > 0:
                emit_scores(gkc)
            P.wait("s", bS, sc_val[gkc])
            if gkc == 0:
                P.wait("s", sM, nm_rt)
            P.use("s", ("es", gkc % 4))
            exp_val[gkc] = P.emit("s", lambda s, gkc=gkc: s.activation(
                out=es[:, gkc % 4, :], in_=pS2[gkc % 2][:], func=AF.Exp,
                scale=0.125, bias=nm_s[:]), fS)
            P.guard(f"bank{b0_}", fS)
            P.guard(f"bank{b0_ + 1}", fS)
            if gkc >= 3:
                emit_avpair(gkc - 3)
                if (gkc - 3) % KT == KT - 1:
                    emit_norm((gkc - 3) // KT)
        for gkc in range(QCN * KT - 3, QCN * KT):
            emit_avpair(gkc)
            if gkc % KT == KT - 1:
                emit_norm(gkc // KT)
        P.guard("nm_s", fS)
    attn_stg_dR = dR.n

    P.wait("t", dWm, misc_dW)

    def wo_stage(qc):
        p0, p1 = (4, 5) if qc % 2 == 0 else (6, 7)
        yp = qc % 2
        stat_pend = []

        def emit_stats(m, fy_v, sm_v):
            if m == 0:
                P.use("t", f"bank{p0}")
                P.use("t", f"bank{p1}")
            P.wait("t", fY, fy_v)
            P.emit("t", lambda t, m=m, p0=p0, yp=yp: t.matmul(
                pb[p0][0:1, :], ones_c[:], yres[:, yp, m, :],
                start=(m == 0), stop=(m == KC - 1)))
            P.wait("t", sM, sm_v)
            P.emit("t", lambda t, m=m, p1=p1: t.matmul(
                pb[p1][0:1, :], ones_c[:], y2f[:, m, :],
                start=(m == 0), stop=(m == KC - 1)), bST)
            P.guard(("y2f", m), bST)

        for m in range(KC):
            bank = m % 2
            P.use("t", f"bank{bank}")
            if qc == 0 and m == 0:
                P.wait("t", fAV, fAV.n)
                P.wait("t", dR, attn_stg_dR)
            for kc in range(KC):
                P.emit("t", lambda t, bank=bank, m=m, kc=kc, qc=qc: t.matmul(
                    pb[bank][:], wo_s[:, kc, m, :],
                    attnT[:, kc, qc * 512:(qc + 1) * 512],
                    start=(kc == 0), stop=(kc == KC - 1)),
                    bY if kc == KC - 1 else None)
            P.wait("v", bY, bY.n)
            P.use("v", ("yres", yp, m))
            P.emit("v", lambda v, bank=bank, m=m, qc=qc, yp=yp: v.scalar_tensor_tensor(
                out=yres[:, yp, m, :], in0=pb[bank][:], scalar=bo_s[:, m:m + 1],
                in1=xT[:, 0, m, qc * 512:(qc + 1) * 512],
                op0=ALU.add, op1=ALU.add), fY)
            P.guard(f"bank{bank}", fY)
            P.wait("s", fY, fY.n)
            P.use("s", ("y2f", m))
            P.emit("s", lambda s, m=m, yp=yp: s.activation(
                out=y2f[:, m, :], in_=yres[:, yp, m, :], func=AF.Square), sM)
            stat_pend.append((m, fY.n, sM.n))
            if len(stat_pend) >= 2:
                emit_stats(*stat_pend.pop(0))
        while stat_pend:
            emit_stats(*stat_pend.pop(0))
        return bST.n

    def ln1_stage(qc, bst_val):
        p0, p1 = (4, 5) if qc % 2 == 0 else (6, 7)
        yp = qc % 2
        P.wait("v", bST, bst_val)
        P.use("v", "rows")
        P.emit("v", lambda v, p0=p0: v.tensor_scalar(
            out=rowT[:], in0=pb[p0][0:1, :], scalar1=1.0 / D, scalar2=None,
            op0=ALU.mult), vM)
        P.emit("v", lambda v, p1=p1: v.tensor_scalar(
            out=rowA[:], in0=pb[p1][0:1, :], scalar1=1.0 / D, scalar2=None,
            op0=ALU.mult), vM)
        vfence()
        P.emit("v", lambda v: v.tensor_tensor(
            out=rowB[:], in0=rowT[:], in1=rowT[:], op=ALU.mult), vM)
        vfence()
        P.emit("v", lambda v: v.tensor_tensor(
            out=rowA[:], in0=rowA[:], in1=rowB[:], op=ALU.subtract), vM)
        P.guard(f"bank{p0}", vM)
        P.guard(f"bank{p1}", vM)
        P.wait("s", vM, vM.n)
        P.emit("s", lambda s: s.activation(
            out=rowA[:], in_=rowA[:], func=AF.Ln, bias=eps_s[0:1, :]), sM)
        sfence()
        P.emit("s", lambda s: s.activation(
            out=rowA[:], in_=rowA[:], func=AF.Exp, scale=-0.5), sM)
        P.wait("v", sM, sM.n)
        P.emit("v", lambda v: v.scalar_tensor_tensor(
            out=rowB[:], in0=rowT[:], scalar=-1.0, in1=rowA[:],
            op0=ALU.mult, op1=ALU.mult), vM)
        P.use("t", "bank2")
        P.use("t", "bank3")
        P.wait("t", sM, sM.n)
        P.wait("t", vM, vM.n)
        P.emit("t", lambda t: t.matmul(
            pb[2][:], ones32r[:], rowA[:], start=True, stop=True))
        P.emit("t", lambda t: t.matmul(
            pb[3][:], ones32r[:], rowB[:], start=True, stop=True), bK)
        P.guard("rows", bK, bK.n)
        P.wait("v", bK, bK.n)
        for m in range(KC):
            P.emit("v", lambda v, m=m, qc=qc, yp=yp: v.tensor_tensor(
                out=x1T[:, m, qc * 512:(qc + 1) * 512], in0=yres[:, yp, m, :],
                in1=pb[2][:], op=ALU.mult),
                vM if m == KC - 1 else None)
            P.guard(("yres", yp, m), vM, vM.n)
        vfence()
        for m in range(KC):
            P.emit("v", lambda v, m=m, qc=qc: v.tensor_tensor(
                out=x1T[:, m, qc * 512:(qc + 1) * 512],
                in0=x1T[:, m, qc * 512:(qc + 1) * 512],
                in1=pb[3][:], op=ALU.add), fX1)
        P.guard("bank2", fX1)
        P.guard("bank3", fX1)

    ln1_pend = None
    for qc in range(QCN if stop >= 3 else 0):
        bv_ = wo_stage(qc)
        if ln1_pend is not None:
            ln1_stage(*ln1_pend)
        ln1_pend = (qc, bv_)
    if ln1_pend is not None:
        ln1_stage(*ln1_pend)


    x1_done = fX1.n
    P.wait("t", fX1, x1_done)
    for qc in range(QCN if stop >= 4 else 0):
        P.use("t", "bank6")
        for kc in range(KC):
            P.emit("t", lambda t, kc=kc, qc=qc: t.matmul(
                pb[6][0:7, :], rw_s[:, kc, 0:7],
                x1T[:, kc, qc * 512:(qc + 1) * 512],
                start=(kc == 0), stop=(kc == KC - 1)),
                bK if kc == KC - 1 else None)
        P.wait("s", bK, bK.n)
        P.use("s", "lgs")
        P.emit("s", lambda s: s.copy(out=lgs[:], in_=pb[6][0:7, :]), sM)
        P.guard("bank6", sM)
        for ts in range(4):
            bank = ts % 2
            P.use("t", f"bank{bank}")
            P.wait("t", sM, sM.n)
            P.emit("t", lambda t, ts=ts, bank=bank: t.transpose(
                pT32[:, bank * 512:bank * 512 + 7],
                lgs[0:7, ts * 128:(ts + 1) * 128], id32[0:7, 0:7]), tA)
            P.wait("v", tA, tA.n)
            P.emit("v", lambda v, ts=ts, qc=qc, bank=bank: v.tensor_copy(
                out=lgt[:, qc * 4 + ts, 0:7], in_=pT32[:, bank * 512:bank * 512 + 7]), eT)
            P.guard(f"bank{bank}", eT)
        P.guard("lgs", eT)
    P.wait("s", eT, eT.n)
    for i in range(NT if stop >= 4 else 0):
        P.emit("s", lambda s, i=i: s.activation(
            out=prb[:, i, 0:7], in_=lgt[:, i, 0:7], func=AF.Exp,
            accum_out=scs[:, i, 7:8]), sM)
    P.wait("v", sM, sM.n)

    def vstep(build_fn):
        for i in range(NT if stop >= 4 else 0):
            P.emit("v", build_fn(i), vM if i == NT - 1 else None)
        vfence()

    vstep(lambda i: (lambda v, i=i: v.reciprocal(
        out=scs[:, i, 6:7], in_=scs[:, i, 7:8])))
    vstep(lambda i: (lambda v, i=i: v.tensor_scalar(
        out=prb[:, i, 0:7], in0=prb[:, i, 0:7], scalar1=scs[:, i, 6:7],
        scalar2=None, op0=ALU.mult)))
    vstep(lambda i: (lambda v, i=i: v.reduce_max(
        out=scs[:, i, 0:1], in_=prb[:, i, 0:7], axis=AX.X)))
    vstep(lambda i: (lambda v, i=i: v.tensor_scalar(
        out=msk[:, i, 0:7], in0=prb[:, i, 0:7], scalar1=scs[:, i, 0:1],
        scalar2=None, op0=ALU.is_ge)))
    vstep(lambda i: (lambda v, i=i: v.reduce_sum(
        out=scs[:, i, 1:2], in_=msk[:, i, 0:7], axis=AX.X)))
    vstep(lambda i: (lambda v, i=i: v.scalar_tensor_tensor(
        out=msk[:, i, 0:7], in0=msk[:, i, 0:7], scalar=-1e30, in1=prb[:, i, 0:7],
        op0=ALU.mult, op1=ALU.add)))
    vstep(lambda i: (lambda v, i=i: v.reduce_max(
        out=scs[:, i, 2:3], in_=msk[:, i, 0:7], axis=AX.X)))
    vstep(lambda i: (lambda v, i=i: v.tensor_scalar(
        out=scs[:, i, 3:4], in0=scs[:, i, 1:2], scalar1=1.5, scalar2=None,
        op0=ALU.is_ge)))
    vstep(lambda i: (lambda v, i=i: v.tensor_tensor(
        out=scs[:, i, 4:5], in0=scs[:, i, 0:1], in1=scs[:, i, 2:3], op=ALU.subtract)))
    vstep(lambda i: (lambda v, i=i: v.tensor_tensor(
        out=scs[:, i, 4:5], in0=scs[:, i, 4:5], in1=scs[:, i, 3:4], op=ALU.mult)))
    vstep(lambda i: (lambda v, i=i: v.tensor_tensor(
        out=scs[:, i, 4:5], in0=scs[:, i, 2:3], in1=scs[:, i, 4:5], op=ALU.add)))
    vstep(lambda i: (lambda v, i=i: v.tensor_scalar(
        out=msk[:, i, 0:7], in0=prb[:, i, 0:7], scalar1=scs[:, i, 4:5],
        scalar2=None, op0=ALU.is_ge)))
    vstep(lambda i: (lambda v, i=i: v.tensor_scalar(
        out=scs[:, i, 5:6], in0=scs[:, i, 0:1], scalar1=scs[:, i, 4:5],
        scalar2=1e-9, op0=ALU.add, op1=ALU.add)))
    vstep(lambda i: (lambda v, i=i: v.reciprocal(
        out=scs[:, i, 6:7], in_=scs[:, i, 5:6])))
    vstep(lambda i: (lambda v, i=i: v.tensor_tensor(
        out=comb[:, i, 0:7], in0=prb[:, i, 0:7], in1=msk[:, i, 0:7], op=ALU.mult)))
    vstep(lambda i: (lambda v, i=i: v.tensor_scalar(
        out=comb[:, i, 0:7], in0=comb[:, i, 0:7], scalar1=scs[:, i, 6:7],
        scalar2=None, op0=ALU.mult)))
    comb_vM = vM.n
    P.wait("t", vM, comb_vM)
    for tt in range(NT if stop >= 4 else 0):
        bank = tt % 2
        P.use("t", f"bank{bank}")
        P.emit("t", lambda t, tt=tt, bank=bank: t.transpose(
            pT32[0:7, bank * 512:bank * 512 + 128], comb[:, tt, 0:7], id32[:]), tA)
        P.wait("v", tA, tA.n)
        P.emit("v", lambda v, tt=tt, bank=bank: v.tensor_copy(
            out=combT[0:7, tt * 128:(tt + 1) * 128],
            in_=pT32[0:7, bank * 512:bank * 512 + 128]), eT)
        P.guard(f"bank{bank}", eT)
    if stop >= 4:
        P.wait("y", eT, eT.n)
        P.wait("y", dR, dR.n)
        P.emit("y", lambda y: y.dma_start(out=comb_d, in_=combT[0:7, :]), dR, 16)
        P.wait("y", dR, dR.n)
    comb_dR = dR.n

    for par in range(2):
        P.guards[("wg", par)] = (fX1, x1_done)
        P.guards[("wu", par)] = (fX1, x1_done)
        P.guards[("wd", par)] = (fX1, x1_done)
    pend = None
    hpar_ctr = 0

    def emit_down(par, hpar, fh_val, wd_val, e, fq, tc):
        P.wait("t", fH, fh_val)
        P.wait("t", dWd[par], wd_val)
        for dm in range(KC):
            ob = 4 + dm % 4
            P.use("t", f"bank{ob}")
            for fsub in range(4):
                P.emit("t", lambda t, ob=ob, par=par, hpar=hpar, fsub=fsub, dm=dm:
                       t.matmul(pb[ob][:],
                                wd[:, par, fsub, dm * 128:(dm + 1) * 128],
                                hbuf[:, hpar, fsub, :],
                                start=(fsub == 0), stop=(fsub == 3)),
                       bO if fsub == 3 else None)
            P.wait("v", bO, bO.n)
            P.use("v", ("accs", dm, tc))
            if e == 0 and fq == 0:
                P.emit("v", lambda v, ob=ob, dm=dm, tc=tc: v.tensor_copy(
                    out=acc[:, dm, tc * 512:(tc + 1) * 512], in_=pb[ob][:]), fO)
            else:
                P.emit("v", lambda v, ob=ob, dm=dm, tc=tc: v.tensor_tensor(
                    out=acc[:, dm, tc * 512:(tc + 1) * 512],
                    in0=acc[:, dm, tc * 512:(tc + 1) * 512], in1=pb[ob][:],
                    op=ALU.add), fO)
            P.guard(("accs", dm, tc), fO)
            P.guard(f"bank{ob}", fO)
        for fsub in range(4):
            P.guard(("hbuf", hpar, fsub), bO)
        P.guard(("wd", par), bO)

    for e in range(NEXP if stop >= 5 else 0):
        if e > 0:
            cbp = e % 2
            P.wait("g", dR, comb_dR)
            for tc in range(TCN):
                P.use("g", ("cb", cbp, tc))
                g_dma(cb_all[:, cbp, tc, :],
                      bass.AP(comb_d.tensor, (e - 1) * TOK + tc * 512, [[0, 128], [1, 512]]),
                      dWc)
            for tc in range(TCN):
                P.guards[("cb_ld", cbp, tc)] = (dWc, dWc.n)
        for fq in range(FQN):
            par = fq % 2
            P.use("g", ("wg", par))
            g_dma(wg[:, par, :, :], gt[e, :, :, fq * 512:(fq + 1) * 512], dWg[par])
            P.use("g", ("wu", par))
            g_dma(wu[:, par, :, :], ut[e, :, :, fq * 512:(fq + 1) * 512], dWu[par])
            P.use("g", ("wd", par))
            g_dma(wd[:, par, :, :], dtw[e, :, 4 * fq:4 * fq + 4, :], dWd[par])
            P.wait("t", dWg[par], dWg[par].n)
            P.wait("t", dWu[par], dWu[par].n)
            if e == 0 and fq == 0:
                P.wait("t", fX1, x1_done)
            for tc in range(TCN):
                hpar = hpar_ctr % 2
                hpar_ctr += 1
                for fsub in range(4):
                    gb, ub = fsub % 2, 2 + fsub % 2
                    P.use("t", f"bank{gb}")
                    for kc in range(KC):
                        P.emit("t", lambda t, gb=gb, par=par, kc=kc, fsub=fsub, tc=tc:
                               t.matmul(pb[gb][:],
                                        wg[:, par, kc, fsub * 128:(fsub + 1) * 128],
                                        x1T[:, kc, tc * 512:(tc + 1) * 512],
                                        start=(kc == 0), stop=(kc == KC - 1)),
                               bG if kc == KC - 1 else None)
                    P.use("t", f"bank{ub}")
                    for kc in range(KC):
                        P.emit("t", lambda t, ub=ub, par=par, kc=kc, fsub=fsub, tc=tc:
                               t.matmul(pb[ub][:],
                                        wu[:, par, kc, fsub * 128:(fsub + 1) * 128],
                                        x1T[:, kc, tc * 512:(tc + 1) * 512],
                                        start=(kc == 0), stop=(kc == KC - 1)),
                               bU if kc == KC - 1 else None)
                    P.wait("s", bG, bG.n)
                    P.use("s", ("s1b", fsub))
                    P.emit("s", lambda s, gb=gb, fsub=fsub: s.activation(
                        out=s1b[:, fsub, :], in_=pb[gb][:], func=AF.Silu), sS1)
                    P.guard(f"bank{gb}", sS1)
                    P.wait("v", bU, bU.n)
                    P.wait("v", sS1, sS1.n)
                    if e == 0:
                        P.use("v", ("hbuf", hpar, fsub))
                        P.emit("v", lambda v, ub=ub, hpar=hpar, fsub=fsub: v.tensor_tensor(
                            out=hbuf[:, hpar, fsub, :], in0=s1b[:, fsub, :],
                            in1=pb[ub][:], op=ALU.mult), fH)
                        P.guard(("s1b", fsub), fH)
                    else:
                        if fq == 0 and fsub == 0:
                            P.wait("v", dWc, P.guards[("cb_ld", e % 2, tc)][1])
                        P.use("v", ("t1b", fsub))
                        P.emit("v", lambda v, ub=ub, fsub=fsub: v.tensor_tensor(
                            out=t1b[:, fsub, :], in0=s1b[:, fsub, :],
                            in1=pb[ub][:], op=ALU.mult), fH)
                        P.guard(("s1b", fsub), fH)
                    P.guard(f"bank{ub}", fH)
                if pend is not None:
                    emit_down(*pend)
                if e > 0:
                    P.wait("v", fH, fH.n)
                    for fsub in range(4):
                        P.use("v", ("hbuf", hpar, fsub))
                        P.emit("v", lambda v, hpar=hpar, fsub=fsub, tc=tc, e=e:
                               v.tensor_tensor(
                            out=hbuf[:, hpar, fsub, :], in0=t1b[:, fsub, :],
                            in1=cb_all[:, e % 2, tc, :], op=ALU.mult), fH)
                        P.guard(("t1b", fsub), fH)
                pend = (par, hpar, fH.n, dWd[par].n, e, fq, tc)
            P.guard(("wg", par), bG)
            P.guard(("wu", par), bU)
        if e > 0:
            for tc in range(TCN):
                P.guard(("cb", e % 2, tc), fH)
    if pend is not None:
        emit_down(*pend)
    ffn_fO = fO.n
    otile_box = [0]

    def z_stage(qc):
        p0, p1 = (4, 5) if qc % 2 == 0 else (6, 7)
        zp = qc % 2
        P.wait("v", fO, ffn_fO)
        for m in range(KC):
            P.use("v", ("zbuf", zp, m))
            P.emit("v", lambda v, m=m, qc=qc, zp=zp: v.tensor_tensor(
                out=zbuf[:, zp, m, :], in0=acc[:, m, qc * 512:(qc + 1) * 512],
                in1=x1T[:, m, qc * 512:(qc + 1) * 512], op=ALU.add), vM)
            P.wait("s", vM, vM.n)
            P.use("s", ("z2f", m))
            P.emit("s", lambda s, m=m, zp=zp: s.activation(
                out=z2f[:, m, :], in_=zbuf[:, zp, m, :], func=AF.Square), sM)
            if m == 0:
                P.use("t", f"bank{p0}")
                P.use("t", f"bank{p1}")
            P.wait("t", vM, vM.n)
            P.emit("t", lambda t, m=m, p0=p0, zp=zp: t.matmul(
                pb[p0][0:1, :], ones_c[:], zbuf[:, zp, m, :],
                start=(m == 0), stop=(m == KC - 1)))
            P.wait("t", sM, sM.n)
            P.emit("t", lambda t, m=m, p1=p1: t.matmul(
                pb[p1][0:1, :], ones_c[:], z2f[:, m, :],
                start=(m == 0), stop=(m == KC - 1)), bST)
            P.guard(("z2f", m), bST)
        return bST.n

    def ln2_stage(qc, bst_val):
        p0, p1 = (4, 5) if qc % 2 == 0 else (6, 7)
        zp = qc % 2
        P.wait("v", bST, bst_val)
        P.use("v", "rows")
        P.emit("v", lambda v, p0=p0: v.tensor_scalar(
            out=rowT[:], in0=pb[p0][0:1, :], scalar1=1.0 / D, scalar2=None,
            op0=ALU.mult), vM)
        P.emit("v", lambda v, p1=p1: v.tensor_scalar(
            out=rowA[:], in0=pb[p1][0:1, :], scalar1=1.0 / D, scalar2=None,
            op0=ALU.mult), vM)
        vfence()
        P.emit("v", lambda v: v.tensor_tensor(
            out=rowB[:], in0=rowT[:], in1=rowT[:], op=ALU.mult), vM)
        vfence()
        P.emit("v", lambda v: v.tensor_tensor(
            out=rowA[:], in0=rowA[:], in1=rowB[:], op=ALU.subtract), vM)
        P.guard(f"bank{p0}", vM)
        P.guard(f"bank{p1}", vM)
        P.wait("s", vM, vM.n)
        P.emit("s", lambda s: s.activation(
            out=rowA[:], in_=rowA[:], func=AF.Ln, bias=eps_s[0:1, :]), sM)
        sfence()
        P.emit("s", lambda s: s.activation(
            out=rowA[:], in_=rowA[:], func=AF.Exp, scale=-0.5), sM)
        P.wait("v", sM, sM.n)
        P.emit("v", lambda v: v.scalar_tensor_tensor(
            out=rowB[:], in0=rowT[:], scalar=-1.0, in1=rowA[:],
            op0=ALU.mult, op1=ALU.mult), vM)
        P.use("t", "bank2")
        P.use("t", "bank3")
        P.wait("t", sM, sM.n)
        P.wait("t", vM, vM.n)
        for ts in range(4):
            P.emit("t", lambda t, ts=ts: t.transpose(
                pb[2][:, ts:ts + 1], rowA[0:1, ts * 128:(ts + 1) * 128],
                id32[0:1, 0:1]))
            P.emit("t", lambda t, ts=ts: t.transpose(
                pb[3][:, ts:ts + 1], rowB[0:1, ts * 128:(ts + 1) * 128],
                id32[0:1, 0:1]), bK if ts == 3 else None)
        P.guard("rows", bK, bK.n)
        P.wait("v", bK, bK.n)
        P.use("v", "cols")
        P.emit("v", lambda v: v.tensor_copy(out=colsA[:], in_=pb[2][:, 0:4]), vM)
        P.emit("v", lambda v: v.tensor_copy(out=colsB[:], in_=pb[3][:, 0:4]), vM)
        P.guard("bank2", vM)
        P.guard("bank3", vM)
        P.wait("s", vM, vM.n)
        for ts in range(4):
            otile = otile_box[0]
            tp = ts % 2
            P.use("t", f"bank{tp}")
            for m in range(KC):
                P.emit("t", lambda t, m=m, ts=ts, zp=zp, tp=tp: t.transpose(
                    pTb[:, tp * 1024 + m * 128:tp * 1024 + (m + 1) * 128],
                    zbuf[:, zp, m, ts * 128:(ts + 1) * 128], idbf[:]),
                    tA if m == KC - 1 else None)
            P.wait("s", tA, tA.n)
            P.use("s", ("outb", otile % 2))
            P.emit("s", lambda s, ot=otile, ts=ts, tp=tp: s.activation(
                out=outb[:, ot % 2, :], in_=pTb[:, tp * 1024:(tp + 1) * 1024],
                func=AF.Identity, scale=colsA[:, ts:ts + 1],
                bias=colsB[:, ts:ts + 1]), eT)
            P.guard(f"bank{tp}", eT)
            P.wait("g", eT, eT.n)
            P.emit("g", lambda g, ot=otile: g.dma_start(
                out=out[ot * 128:(ot + 1) * 128, :], in_=outb[:, ot % 2, :]),
                dO[otile % 2], 16)
            P.guards[("outb", otile % 2)] = (dO[otile % 2], dO[otile % 2].n)
            otile_box[0] += 1
        for m in range(KC):
            P.guard(("zbuf", zp, m), tA)
        P.guard("cols", eT)

    ln2_pend = None
    for qc in range(QCN if stop >= 6 else 0):
        bv_ = z_stage(qc)
        if ln2_pend is not None:
            ln2_stage(*ln2_pend)
        ln2_pend = (qc, bv_)
    if ln2_pend is not None:
        ln2_stage(*ln2_pend)


    P.wait("g", dO[0], dO[0].n)
    P.wait("g", dO[1], dO[1].n)
    P.wait("y", dO[0], dO[0].n)
    P.wait("y", dO[1], dO[1].n)

    with nc.Block() as block:
        @block.tensor
        def _(t):
            for fn in P.q["t"]:
                fn(t)

        @block.vector
        def _(v):
            for fn in P.q["v"]:
                fn(v)

        @block.scalar
        def _(s):
            for fn in P.q["s"]:
                fn(s)

        @block.gpsimd
        def _(g):
            for fn in P.q["g"]:
                fn(g)

        @block.sync
        def _(y):
            for fn in P.q["y"]:
                fn(y)

    return nc


def _pack_inputs(L0_attn_x, L0_attn_Wqkv, L0_attn_bqkv, L0_attn_Wo, L0_attn_bo,
                 L0_ffn_shared_up, L0_ffn_shared_gate, L0_ffn_shared_down,
                 L0_ffn_routed_up, L0_ffn_routed_gate, L0_ffn_routed_down,
                 L0_ffn_router_w):
    f32 = np.float32
    W = np.asarray(L0_attn_Wqkv, f32)
    bqkv = np.asarray(L0_attn_bqkv, f32)
    Wo = np.asarray(L0_attn_Wo, f32)
    bo = np.asarray(L0_attn_bo, f32)
    x = np.asarray(L0_attn_x, f32)

    hidx = np.arange(16)
    q_rows = (hidx[:, None] * 192 + np.arange(64)[None, :]).ravel()
    k_rows = q_rows + 64
    v_rows = q_rows + 128
    wqkv_t = np.empty((128, PAIRS, 3, KC, 128), f32)
    for j, rows in enumerate((q_rows, k_rows, v_rows)):
        Wt = W[rows].T.reshape(KC, 128, PAIRS, 128)
        wqkv_t[:, :, j, :, :] = Wt.transpose(1, 2, 0, 3)
    bq = bqkv[q_rows]; bk = bqkv[k_rows]; bv = bqkv[v_rows]
    bq_c = np.ascontiguousarray(bq.reshape(8, 128).T)
    bk_c = np.ascontiguousarray(bk.reshape(8, 128).T)
    bo_c = np.ascontiguousarray(bo.reshape(8, 128).T)
    bv_row = np.ascontiguousarray(bv.reshape(1, D))
    WoT = Wo.T.reshape(KC, 128, 8, 128).transpose(1, 0, 2, 3)
    wo_t = np.ascontiguousarray(WoT.reshape(128, KC * 8 * 128))

    G = np.concatenate([np.asarray(L0_ffn_shared_gate, f32)[None],
                        np.asarray(L0_ffn_routed_gate, f32)], 0)
    U = np.concatenate([np.asarray(L0_ffn_shared_up, f32)[None],
                        np.asarray(L0_ffn_routed_up, f32)], 0)
    Dn = np.concatenate([np.asarray(L0_ffn_shared_down, f32)[None],
                         np.asarray(L0_ffn_routed_down, f32)], 0)
    gt = np.ascontiguousarray(G.transpose(0, 2, 1).reshape(NEXP, KC, 128, DFF)
                              .transpose(0, 2, 1, 3))
    ut = np.ascontiguousarray(U.transpose(0, 2, 1).reshape(NEXP, KC, 128, DFF)
                              .transpose(0, 2, 1, 3))
    dtw = np.ascontiguousarray(Dn.transpose(0, 2, 1).reshape(NEXP, 32, 128, D)
                               .transpose(0, 2, 1, 3))
    rw = np.asarray(L0_ffn_router_w, f32)
    rw_t = np.ascontiguousarray(rw.T.reshape(KC, 128, 7).transpose(1, 0, 2))

    shared = dict(wqkv_t=wqkv_t, wo_t=wo_t, bq_c=bq_c, bk_c=bk_c, bo_c=bo_c,
                  bv_row=bv_row, ident32=np.eye(128, dtype=f32),
                  gt=gt, ut=ut, dtw=dtw, rw_t=rw_t)
    in_maps = []
    for c in range(8):
        b, h = c // 2, c % 2
        xb = x[b]
        if h == 0:
            xr = np.ascontiguousarray(xb)
        else:
            xr = np.ascontiguousarray(np.concatenate([xb[TOK:], xb[:TOK]], 0))
        in_maps.append(dict(shared, xkv=xr))
    return in_maps


def kernel(**inputs):
    in_maps = _pack_inputs(**inputs)
    nc = build_nc()
    res = run_bass_kernel_spmd(nc, in_maps, core_ids=list(range(8)))
    full = np.empty((B, N, D), np.float32)
    for c in range(8):
        full[c // 2, (c % 2) * TOK:(c % 2 + 1) * TOK] = res.results[c]["out"]
    return full
